# version 04
# baseline (speedup 1.0000x reference)
import math
import contextlib
from collections import defaultdict

import numpy as np
import ml_dtypes

import concourse.bass as bass
import concourse.mybir as mybir
from concourse.bass_utils import run_bass_kernel_spmd

F32 = mybir.dt.float32
BF16 = mybir.dt.bfloat16
AF = mybir.ActivationFunctionType
ALU = mybir.AluOpType
AX = mybir.AxisListType
BFNP = ml_dtypes.bfloat16

E = 1024
EC = 8
HF = 16
DF = 64
DEXP = 2048
NE = 8
LN_EPS = 1e-5
ATTN_EPS = 1e-5
EXP_SHIFT = -10.0
SM_SCALE = 0.125
EPS_SH = ATTN_EPS * math.exp(EXP_SHIFT)
LN_HALF = math.log(0.5)

_slopes = 2.0 ** np.linspace(0.0, -8.0, 8)
LAMB = [float(np.exp(-s)) for s in _slopes] + [1.0] * 8
LAMC = [l ** 128 for l in LAMB]

ENG2BLK = {"pe": "tensor", "act": "scalar", "dve": "vector", "sp": "sync", "gp": "gpsimd"}


def make_consts(T):
    TI = T // 128
    inv = 1.0 / (10000.0 ** (np.arange(0, DF, 2, dtype=np.float64) / DF))
    pos = np.arange(T, dtype=np.float64)
    fr = np.outer(pos, inv)
    emb = np.concatenate([fr, fr], axis=1)
    cos = np.cos(emb)
    sinf = np.sin(emb)
    sinf[:, :32] *= -1.0

    def tab(a):
        return np.ascontiguousarray(
            a.reshape(TI, 128, DF).transpose(1, 0, 2).reshape(128, TI * DF)
        ).astype(np.float32)

    jj = np.arange(128, dtype=np.float64)[:, None]
    ii = np.arange(128, dtype=np.float64)[None, :]
    masks = []
    lamqf = []
    with np.errstate(under="ignore"):
        for h in range(16):
            lam = LAMB[h]
            masks.append(np.where(ii >= jj, lam ** np.maximum(ii - jj, 0.0), 0.0))
            lamqf.append(np.broadcast_to(lam ** (np.arange(128, dtype=np.float64) + 1), (128, 128)))
        maskT = np.concatenate(masks, axis=1).astype(np.float32)
        lamq_feat = np.concatenate(lamqf, axis=1).astype(np.float32)
        lamk = np.stack(
            [LAMB[h] ** (127 - np.arange(128, dtype=np.float64)) for h in range(16)], axis=1
        ).astype(np.float32)
    ident = np.eye(128, dtype=np.float32)
    cpool = np.zeros((128, 8), np.float32)
    for j, vv in enumerate([LN_EPS, -0.5, LN_HALF, SM_SCALE, EXP_SHIFT, -1.0, 0.0, 1.0]):
        cpool[:, j] = vv
    return {
        "cos_t": tab(cos), "sinf_t": tab(sinf), "maskT": maskT,
        "lamq_feat": lamq_feat, "lamk": lamk, "ident": ident, "cpool": cpool,
    }


ENG_SEM = {"pe": "pe", "act": "act", "dve": "dve", "sp": "dx", "gp": "dg"}
ENG_INC = {"pe": 1, "act": 1, "dve": 1, "sp": 16, "gp": 16}


class Em:
    def __init__(self, nc, paranoid=()):
        self.nc = nc
        self.pset = paranoid if not isinstance(paranoid, bool) else (
            {"a1","a2","a3","m0","mp","ms","c1","c2","c3","tap","glob"} if paranoid else set())
        self.paranoid = False
        self.sems = {}
        self.cnt = defaultdict(int)
        self.mark = {}
        self.ops = None

    def phase(self, tag):
        self.paranoid = tag in self.pset

    def sem(self, name):
        if name not in self.sems:
            self.sems[name] = self.nc.alloc_semaphore(name)
        return self.sems[name]

    def begin(self):
        assert self.ops is None
        self.ops = {e: [] for e in ENG2BLK}
        self._fence = [("dx", self.cnt["dx"]), ("dg", self.cnt["dg"])]

    def o(self, eng, fn, w=(), i=()):
        w = list(w)
        i = list(i)
        if self.paranoid:
            w = [(s, self.cnt[s]) for s in ("pe", "act", "dve", "dx", "dg")]
            if not any(s == ENG_SEM[eng] for s, _ in i):
                i = i + [(ENG_SEM[eng], ENG_INC[eng])]
        for s, d in i:
            self.cnt[s] += d
        self.ops[eng].append((w, fn, i))

    def c(self, s):
        return self.cnt[s]

    def _validate(self, ops):
        import copy
        counts = dict(self._vcounts) if hasattr(self, "_vcounts") else {}
        heads = {e: 0 for e in ops}
        total = sum(len(l) for l in ops.values())
        done = 0
        while done < total:
            progressed = False
            for e, lst in ops.items():
                while heads[e] < len(lst):
                    waits, fn, incs = lst[heads[e]]
                    if all(counts.get(s, 0) >= v for s, v in waits):
                        for s, d in incs:
                            counts[s] = counts.get(s, 0) + d
                        heads[e] += 1
                        done += 1
                        progressed = True
                    else:
                        break
            if not progressed:
                msg = []
                for e, lst in ops.items():
                    if heads[e] < len(lst):
                        waits, fn, incs = lst[heads[e]]
                        bad = [(s, v, counts.get(s, 0)) for s, v in waits
                               if counts.get(s, 0) < v]
                        co = fn.__code__
                        msg.append(f"{e}[{heads[e]}/{len(lst)}] at "
                                   f"{co.co_filename}:{co.co_firstlineno} waits {bad}")
                raise RuntimeError("plan deadlock:\n" + "\n".join(msg))
        self._vcounts = counts

    def flush(self):
        ops = self.ops
        self.ops = None
        for eng, lst in ops.items():
            if lst:
                w, fn, i = lst[0]
                lst[0] = ([f for f in self._fence if f[1] > 0] + w, fn, i)
        self._validate(ops)
        with self.nc.allow_low_precision(reason="bf16 matmul operands by design"), \
                self.nc.Block() as blk:
            for eng, lst in ops.items():
                if not lst:
                    continue

                def body(e, lst=lst):
                    maxw = {}
                    for waits, fn, incs in lst:
                        for s, v in waits:
                            if v > maxw.get(s, 0):
                                e.wait_ge(self.sem(s), v)
                                maxw[s] = v
                        ins = fn(e)
                        for s, d in incs:
                            ins.then_inc(self.sem(s), d)

                getattr(blk, ENG2BLK[eng])(body)


def build_program(T, taps=(), paranoid=()):
    nc = bass.Bass("TRN2", target_bir_lowering=False, detect_race_conditions=False)
    TI = T // 128
    QB = T // 512
    taps = set(taps)
    dram = {}

    def din(name, shape):
        dram[name] = nc.dram_tensor(name, list(shape), F32, kind="ExternalInput").ap()

    def dscr(name, shape, dtype):
        dram[name] = nc.dram_tensor(name, list(shape), dtype, kind="Internal").ap()

    din("x", (T, E))
    out_f = nc.dram_tensor("out", [T, E], F32, kind="ExternalOutput").ap()
    for n, sh in [
        ("fqw", (E, E)), ("fkw", (E, E)), ("fvw", (E, E)), ("ffw", (E, E)),
        ("few", (NE, E, E)), ("fgw", (E, NE)),
        ("fqb", (1, E)), ("fkb", (1, E)), ("fvb", (1, E)), ("ffb", (1, E)),
        ("fgb", (1, NE)), ("feb", (NE, E)),
        ("mwin", (E, 2 * DEXP)), ("mwq", (E, DEXP)), ("mwk", (E, DEXP)),
        ("mwout", (DEXP, E)), ("mbq", (1, DEXP)), ("mbk", (1, DEXP)),
        ("inlnw", (128, EC)), ("inlnb", (128, EC)), ("qlnw", (128, EC)),
        ("qlnb", (128, EC)), ("klnw", (128, EC)), ("klnb", (128, EC)),
        ("outlnw", (1, E)), ("outlnb", (1, E)),
        ("smear", (1, 16)), ("logscale", (1, 16)),
        ("cos_t", (128, TI * DF)), ("sinf_t", (128, TI * DF)),
        ("maskT", (128, 16 * 128)), ("lamq_feat", (128, 16 * 128)),
        ("lamk", (128, 16)), ("ident", (128, 128)), ("cpool", (128, 8)),
    ]:
        din(n, sh)

    dscr("x1", (T, E), F32)
    dscr("x2", (T, E), F32)
    dscr("qT_s", (EC, 128, T), BF16)
    dscr("kT_s", (EC, 128, T), BF16)
    dscr("v_s", (EC, 128, TI * 130), BF16)
    dscr("aT_s", (EC, 128, T), BF16)
    dscr("y_s", (T, E), BF16)
    for n in ["mqhT", "mkhT", "mktl", "mvtk", "mphT"]:
        dscr(n, (TI, 128, DEXP), BF16)
    if "x1" in taps:
        dram["x1o"] = nc.dram_tensor("x1o", [T, E], F32, kind="ExternalOutput").ap()
    if "x2" in taps:
        dram["x2o"] = nc.dram_tensor("x2o", [T, E], F32, kind="ExternalOutput").ap()
    for tp, srcn in [("aqT", "qT_s"), ("akT", "kT_s"), ("av", "v_s"), ("aaT", "aT_s")]:
        if tp in taps:
            shp = [EC, 128, dram[srcn].shape[2]]
            dram[tp + "o"] = nc.dram_tensor(tp + "o", shp, BF16, kind="ExternalOutput").ap()
    if "ay" in taps:
        dram["ayo"] = nc.dram_tensor("ayo", [T, E], BF16, kind="ExternalOutput").ap()

    em = Em(nc, paranoid=paranoid)
    top = contextlib.ExitStack()

    with top:
        ident_b = top.enter_context(nc.sbuf_tensor("g_ident", [128, 128], BF16))
        onesb = top.enter_context(nc.sbuf_tensor("g_onesb", [1, 128], BF16))
        onesf = top.enter_context(nc.sbuf_tensor("g_onesf", [1, 128], F32))
        onescol = top.enter_context(nc.sbuf_tensor("g_onescol", [128, 1], BF16))
        cpool = top.enter_context(nc.sbuf_tensor("g_cpool", [128, 8], F32))

        nc.const_aps.register(F32, LN_EPS, cpool[:, 0:1])
        nc.const_aps.register(F32, -0.5, cpool[:, 1:2])
        nc.const_aps.register(F32, LN_HALF, cpool[:, 2:3])
        nc.const_aps.register(F32, SM_SCALE, cpool[:, 3:4])
        nc.const_aps.register(F32, EXP_SHIFT, cpool[:, 4:5])
        nc.const_aps.register(F32, -1.0, cpool[:, 5:6])
        nc.const_aps.register(F32, 0.0, cpool[:, 6:7])
        nc.const_aps.register(F32, 1.0, cpool[:, 7:8])

        em.begin()
        em.o("gp", lambda g: g.dma_start(out=ident_b[:, :], in_=dram["ident"][:, :]), i=[("dg", 16)])
        em.o("sp", lambda s: s.dma_start(out=cpool[:, :], in_=dram["cpool"][:, :]), i=[("dx", 16)])
        em.o("dve", lambda v: v.memset(onesb[:, :], 1.0))
        em.o("dve", lambda v: v.memset(onesf[:, :], 1.0))
        em.o("dve", lambda v: v.memset(onescol[:, :], 1.0), i=[("dve", 1)])
        em.flush()

        def ln_chain(xsb, stat, sqscr, rstd_bias, wx):
            mu, musq, ssq, rstd = (stat[:, j:j + 1] for j in range(4))
            em.o("dve", lambda v: v.reduce_sum(mu, xsb[:, :], axis=AX.X), w=wx, i=[("dve", 1)])
            em.o("dve", lambda v: v.tensor_scalar(mu, mu, 1.0 / E, None, ALU.mult),
                 w=[("dve", em.c("dve"))])
            em.o("dve", lambda v: v.scalar_tensor_tensor(
                sqscr[:, :], xsb[:, :], 1.0, xsb[:, :], ALU.mult, ALU.mult, accum_out=ssq),
                 i=[("dve", 1)])
            em.o("dve", lambda v: v.tensor_tensor(musq, mu, mu, ALU.mult))
            em.o("dve", lambda v: v.scalar_tensor_tensor(
                ssq, ssq, 1.0 / E, musq, ALU.mult, ALU.subtract),
                 w=[("dve", em.c("dve"))], i=[("dve", 1)])
            mv = em.c("dve")
            em.o("act", lambda s: s.activation(ssq, ssq, AF.Ln, bias=LN_EPS), w=[("dve", mv)])
            em.o("act", lambda s, bi=rstd_bias: s.activation(rstd, ssq, AF.Exp, scale=-0.5, bias=bi),
                 i=[("act", 1)])
            return em.c("act")

        def transpose_n(src_bf, ptr, dst_sb, n, pe_w=(), dve_w=()):
            key = ("ptrfree", ptr.name)
            pe_w = list(pe_w) + [("dve", em.mark.get(key, 0))]
            for j in range(n):
                em.o("pe", lambda t, o=ptr[:, j * 128:(j + 1) * 128],
                     s=src_bf[:, j * 128:(j + 1) * 128]: t.transpose(o, s, ident_b[:, :]),
                     w=(list(pe_w) if j == 0 else ()), i=([("pe", 1)] if j == n - 1 else []))
            mk_pe = em.c("pe")
            half = n * 128 // 2
            em.o("dve", lambda v: v.tensor_copy(dst_sb[:, 0:half], ptr[:, 0:half]),
                 w=[("pe", mk_pe)] + list(dve_w))
            em.o("dve", lambda v: v.tensor_copy(dst_sb[:, half:n * 128], ptr[:, half:n * 128]),
                 i=[("dve", 1)])
            em.mark[("ptrfree", ptr.name)] = em.c("dve")
            return mk_pe, em.c("dve")

        def transpose16(src_bf, ptr, dst_sb, pe_w=(), dve_w=()):
            mk1, dv1 = transpose_n(src_bf[:, 0:1024], ptr, dst_sb[:, 0:1024], 8, pe_w, dve_w)
            mk2, dv2 = transpose_n(src_bf[:, 1024:2048], ptr, dst_sb[:, 1024:2048], 8,
                                   [("dve", dv1)], ())
            return mk2, dv2

        def proj_mm(psum, yT, wsb, wcols, bias_row, pe_w=()):
            nh = (wcols + 511) // 512
            for half in range(nh):
                w0 = half * 512
                wn = min(512, wcols - w0)
                for c in range(EC):
                    em.o("pe", lambda t, o=psum[:, w0:w0 + wn],
                         l=yT[:, c * 128:(c + 1) * 128],
                         r=wsb[:, c * wcols + w0:c * wcols + w0 + wn],
                         st=(c == 0): t.matmul(o, l, r, start=st, stop=False),
                         w=(list(pe_w) if (half == 0 and c == 0) else ()))
                em.o("pe", lambda t, o=psum[:, w0:w0 + wn], r=bias_row[0:1, w0:w0 + wn]:
                     t.matmul(o, onesb[0:1, 0:128], r, start=False, stop=True),
                     i=([("pe", 1)] if half == nh - 1 else []))
            return em.c("pe")

        def wload(gname, dst_sb, nch, wcols, war=()):
            em.o("gp", lambda g, o=dst_sb, s=dram[gname].rearrange("(c p) f -> p c f", p=128):
                 g.dma_start(out=o[:, :].rearrange("p (c f) -> p c f", c=nch), in_=s),
                 w=list(war), i=[("dg", 16)])
            return em.c("dg")

        def ffn_call(xin_ap, xout_ap, tag):
            em.phase(tag + "1")
            ph = contextlib.ExitStack()
            with ph:
                wq_sb = ph.enter_context(nc.sbuf_tensor(f"{tag}wq", [128, EC * E], BF16))
                wk_sb = ph.enter_context(nc.sbuf_tensor(f"{tag}wk", [128, EC * E], BF16))
                wv_sb = ph.enter_context(nc.sbuf_tensor(f"{tag}wv", [128, EC * E], BF16))
                bq_sb = ph.enter_context(nc.sbuf_tensor(f"{tag}bq", [1, E], BF16))
                bk_sb = ph.enter_context(nc.sbuf_tensor(f"{tag}bk", [1, E], BF16))
                bv_sb = ph.enter_context(nc.sbuf_tensor(f"{tag}bv", [1, E], BF16))
                cos_sb = ph.enter_context(nc.sbuf_tensor(f"{tag}cos", [128, TI * DF], BF16))
                sin_sb = ph.enter_context(nc.sbuf_tensor(f"{tag}sin", [128, TI * DF], BF16))
                xsb = [ph.enter_context(nc.sbuf_tensor(f"{tag}xs{j}", [128, E], F32)) for j in range(2)]
                sqs = ph.enter_context(nc.sbuf_tensor(f"{tag}sqs", [128, E], F32))
                stat = ph.enter_context(nc.sbuf_tensor(f"{tag}st", [128, 4], F32))
                y_bf = [ph.enter_context(nc.sbuf_tensor(f"{tag}yb{j}", [128, E], BF16)) for j in range(2)]
                yT = [ph.enter_context(nc.sbuf_tensor(f"{tag}yT{j}", [128, E], BF16)) for j in range(2)]
                t1 = ph.enter_context(nc.sbuf_tensor(f"{tag}t1", [128, E], F32))
                t2 = ph.enter_context(nc.sbuf_tensor(f"{tag}t2", [128, E], F32))
                rof = [ph.enter_context(nc.sbuf_tensor(f"{tag}ro{j}", [128, E], BF16)) for j in range(2)]
                tT = [ph.enter_context(nc.sbuf_tensor(f"{tag}tT{j}", [128, E], BF16)) for j in range(2)]
                vst = [ph.enter_context(nc.sbuf_tensor(f"{tag}vs{j}", [128, HF * 65], BF16)) for j in range(2)]
                ptr = ph.enter_context(nc.psum_tensor(f"{tag}ptr", [128, E], BF16))
                pq = ph.enter_context(nc.psum_tensor(f"{tag}pq", [128, E], F32))
                pk = ph.enter_context(nc.psum_tensor(f"{tag}pk", [128, E], F32))
                pv = ph.enter_context(nc.psum_tensor(f"{tag}pv", [128, E], F32))

                em.begin()
                wload("fqw", wq_sb, EC, E)
                wload("fkw", wk_sb, EC, E)
                mk_w = wload("fvw", wv_sb, EC, E)
                for bsb, bn in [(bq_sb, "fqb"), (bk_sb, "fkb"), (bv_sb, "fvb")]:
                    em.o("gp", lambda g, o=bsb, s=dram[bn]: g.dma_start(out=o[:, :], in_=s[:, :]),
                         i=[("dg", 16)])
                em.o("gp", lambda g: g.dma_start(out=cos_sb[:, :], in_=dram["cos_t"][:, :]),
                     i=[("dg", 16)])
                em.o("gp", lambda g: g.dma_start(out=sin_sb[:, :], in_=dram["sinf_t"][:, :]),
                     i=[("dg", 16)])
                mk_w = em.c("dg")
                for j in range(2):
                    em.o("dve", lambda v, o=vst[j][:, :].rearrange("p (h c) -> p h c", h=HF)[:, :, 64:65]:
                         v.memset(o, 1.0), i=[("dve", 1)])

                def rope(psum, i_tile, dst_bf, pe_mk, war):
                    c0 = i_tile * DF
                    cos3 = cos_sb[:, c0:c0 + DF][:, None, :].broadcast_to([128, HF, DF])
                    sin_lo = sin_sb[:, c0:c0 + 32][:, None, :].broadcast_to([128, HF, 32])
                    sin_hi = sin_sb[:, c0 + 32:c0 + DF][:, None, :].broadcast_to([128, HF, 32])
                    ps3 = psum[:, :].rearrange("p (h d) -> p h d", h=HF)
                    t23 = t2[:, :].rearrange("p (h d) -> p h d", h=HF)
                    em.o("dve", lambda v: v.scalar_tensor_tensor(
                        t1[:, :].rearrange("p (h d) -> p h d", h=HF), ps3, 1.0, cos3,
                        ALU.mult, ALU.mult), w=[("pe", pe_mk)] + list(war))
                    em.o("dve", lambda v: v.scalar_tensor_tensor(
                        t23[:, :, 0:32], ps3[:, :, 32:64], 1.0, sin_lo, ALU.mult, ALU.mult))
                    em.o("dve", lambda v: v.scalar_tensor_tensor(
                        t23[:, :, 32:64], ps3[:, :, 0:32], 1.0, sin_hi, ALU.mult, ALU.mult))
                    em.o("dve", lambda v: v.tensor_tensor(
                        dst_bf[:, :], t1[:, :], t2[:, :], ALU.add), i=[("dve", 1)])
                    return em.c("dve")

                mk = defaultdict(int)
                for i in range(TI):
                    par = i % 2
                    em.o("sp", lambda s, o=xsb[par], src=dram["x"] if False else xin_ap,
                         sl=slice(i * 128, (i + 1) * 128): s.dma_start(out=o[:, :], in_=src[sl, :]),
                         w=[("dve", mk[f"xfree{par}"])], i=[("dx", 16)])
                    mk_x = em.c("dx")
                    actm = ln_chain(xsb[par], stat, sqs, 0.0, [("dx", mk_x), ("dve", mk[f"xfree{par}"])])
                    em.o("dve", lambda v, o=y_bf[par], a=xsb[par]: v.tensor_scalar(
                         o[:, :], a[:, :], stat[:, 0:1], stat[:, 3:4], ALU.subtract, ALU.mult),
                         w=[("act", actm), ("pe", mk[f"ybusy{par}"])], i=[("dve", 1)])
                    mk[f"xfree{par}"] = em.c("dve")
                    mk_y = em.c("dve")
                    if "ay" in taps:
                        em.o("sp", lambda s, o=dram["y_s"][i * 128:(i + 1) * 128, :],
                             src=y_bf[par]: s.dma_start(out=o, in_=src[:, :]),
                             w=[("dve", mk_y)], i=[("dx", 16)])
                    pem, dvem = transpose_n(y_bf[par], ptr, yT[par], 8,
                                            pe_w=[("dve", mk_y)],
                                            dve_w=[("pe", mk["yT_busy"])])
                    mk[f"ybusy{par}"] = pem
                    pe_q = proj_mm(pq, yT[par], wq_sb, E, bq_sb,
                                   pe_w=[("dve", dvem), ("dg", mk_w), ("dve", mk["ro_q"])])
                    mk["ro_q"] = rope(pq, i, rof[par], pe_q, [("pe", mk["trq"])])
                    pem, dvem_q = transpose_n(rof[par], ptr, tT[par], 8,
                                              pe_w=[("dve", mk["ro_q"])],
                                              dve_w=[("dx", mk[f"stq{par}"])])
                    mk["trq"] = pem
                    em.o("sp", lambda s, o=dram["qT_s"][:, :, i * 128:(i + 1) * 128].transpose([1, 0, 2]),
                         src=tT[par][:, :].rearrange("p (c t) -> p c t", c=EC):
                         s.dma_start(out=o, in_=src), w=[("dve", dvem_q)], i=[("dx", 16)])
                    mk[f"stq{par}"] = em.c("dx")
                    pe_k = proj_mm(pk, yT[par], wk_sb, E, bk_sb, pe_w=[("dve", mk["ro_k"])])
                    mk["ro_k"] = rope(pk, i, rof[par], pe_k, [("pe", mk["trq"]), ("dx", mk[f"stq{par}"])])
                    pem, dvem_k = transpose_n(rof[par], ptr, tT[par], 8,
                                              pe_w=[("dve", mk["ro_k"])],
                                              dve_w=[("dx", mk[f"stq{par}"])])
                    mk["trk"] = pem
                    mk["yT_busy"] = pe_k
                    em.o("sp", lambda s, o=dram["kT_s"][:, :, i * 128:(i + 1) * 128].transpose([1, 0, 2]),
                         src=tT[par][:, :].rearrange("p (c t) -> p c t", c=EC):
                         s.dma_start(out=o, in_=src), w=[("dve", dvem_k)], i=[("dx", 16)])
                    mk[f"stk{par}"] = em.c("dx")
                    pe_v = proj_mm(pv, yT[par], wv_sb, E, bv_sb, pe_w=[("dve", mk["vev"])])
                    mk["yT_busy"] = pe_v
                    em.o("dve", lambda v, o=vst[par][:, :].rearrange("p (h c) -> p h c", h=HF)[:, :, 0:64],
                         s=pv[:, :].rearrange("p (h d) -> p h d", h=HF): v.tensor_copy(o, s),
                         w=[("pe", pe_v), ("dx", mk[f"stv{par}"])], i=[("dve", 1)])
                    mk["vev"] = em.c("dve")
                    em.o("sp", lambda s, o=dram["v_s"][:, :, i * 130:(i + 1) * 130].transpose([1, 0, 2]),
                         src=vst[par][:, :].rearrange("p (c t) -> p c t", c=EC):
                         s.dma_start(out=o, in_=src), w=[("dve", mk["vev"])], i=[("dx", 16)])
                    mk[f"stv{par}"] = em.c("dx")
                em.flush()

            em.phase(tag + "2")
            ph = contextlib.ExitStack()
            with ph:
                kT_p = [ph.enter_context(nc.sbuf_tensor(f"{tag}kp{j}", [128, T], BF16)) for j in range(2)]
                v_p = [ph.enter_context(nc.sbuf_tensor(f"{tag}vp{j}", [128, TI * 130], BF16)) for j in range(2)]
                qTq = [ph.enter_context(nc.sbuf_tensor(f"{tag}qq{j}", [128, 512], BF16)) for j in range(2)]
                expS = [ph.enter_context(nc.sbuf_tensor(f"{tag}eS{j}", [128, 1024], BF16)) for j in range(2)]
                rr = ph.enter_context(nc.sbuf_tensor(f"{tag}rr", [1, 1024], F32))
                rb = ph.enter_context(nc.sbuf_tensor(f"{tag}rb", [1, 1024], BF16))
                rbig = ph.enter_context(nc.sbuf_tensor(f"{tag}rbig", [128, 512], F32))
                aTst = [ph.enter_context(nc.sbuf_tensor(f"{tag}aT{j}", [128, 512], BF16)) for j in range(2)]
                pS = [ph.enter_context(nc.psum_tensor(f"{tag}pS{j}", [128, 1024], F32)) for j in range(3)]
                pAV1 = ph.enter_context(nc.psum_tensor(f"{tag}pA1", [128, 512], F32))
                pAV2 = ph.enter_context(nc.psum_tensor(f"{tag}pA2", [128, 512], F32))

                em.begin()
                mk_kv = [0, 0]
                mk_exp = [0, 0, 0]
                mk_aTT = [0, 0]
                mk_aSt = [0, 0]
                mk_avdone = 0
                mk_rbTT = 0
                step = 0

                def load_kv(p, par):
                    em.o("gp", lambda g, o=kT_p[par], s=dram["kT_s"][p]:
                         g.dma_start(out=o[:, :], in_=s[:, :]), w=[("pe", mk_avdone)], i=[("dg", 16)])
                    em.o("gp", lambda g, o=v_p[par], s=dram["v_s"][p]:
                         g.dma_start(out=o[:, :], in_=s[:, :]), i=[("dg", 16)])
                    mk_kv[par] = em.c("dg")

                load_kv(0, 0)
                for p in range(EC):
                    par = p % 2
                    for qb in range(QB):
                        qpar = qb % 2
                        em.o("sp", lambda s, o=qTq[qpar], src=dram["qT_s"][p][:, qb * 512:(qb + 1) * 512]:
                             s.dma_start(out=o[:, :], in_=src), w=[("pe", mk_avdone)], i=[("dx", 16)])
                        mk_qT = em.c("dx")
                        if qb == 0 and p + 1 < EC:
                            load_kv(p + 1, 1 - par)
                        for tk in range(TI):
                            spar = step % 3
                            epar = step % 2
                            em.o("pe", lambda t, o=pS[spar][:, 0:512],
                                 l=kT_p[par][0:64, tk * 128:(tk + 1) * 128],
                                 r=qTq[qpar][0:64, :]:
                                 t.matmul(o, l, r, start=True, stop=True, tile_position=(0, 0)),
                                 w=[("act", mk_exp[spar]), ("dg", mk_kv[par]), ("dx", mk_qT)])
                            em.o("pe", lambda t, o=pS[spar][:, 512:1024],
                                 l=kT_p[par][64:128, tk * 128:(tk + 1) * 128],
                                 r=qTq[qpar][64:128, :]:
                                 t.matmul(o, l, r, start=True, stop=True, tile_position=(64, 0)),
                                 i=[("pe", 1)])
                            mqk = em.c("pe")
                            em.o("act", lambda s, o=expS[epar], a=pS[spar]:
                                 s.activation(o[:, :], a[:, :], AF.Exp, scale=SM_SCALE, bias=EXP_SHIFT),
                                 w=[("pe", mqk)], i=[("act", 1)])
                            mk_exp[spar] = em.c("act")
                            wav = [("act", mk_exp[spar])]
                            if tk == 0:
                                wav += [("dve", max(mk_aTT))]
                            em.o("pe", lambda t, o=pAV1[0:65, :],
                                 l=v_p[par][:, tk * 130:tk * 130 + 65],
                                 r=expS[epar][:, 0:512], st=(tk == 0), sp=(tk == TI - 1):
                                 t.matmul(o, l, r, start=st, stop=sp), w=wav)
                            em.o("pe", lambda t, o=pAV2[0:65, :],
                                 l=v_p[par][:, tk * 130 + 65:tk * 130 + 130],
                                 r=expS[epar][:, 512:1024], st=(tk == 0), sp=(tk == TI - 1):
                                 t.matmul(o, l, r, start=st, stop=sp),
                                 i=([("pe", 1)] if tk == TI - 1 else []))
                            step += 1
                        mk_avdone = em.c("pe")
                        em.o("dve", lambda v: v.tensor_scalar(
                             rr[0:1, 0:512], pAV1[64:65, 0:512], EPS_SH, None, ALU.add),
                             w=[("pe", mk_avdone)])
                        em.o("dve", lambda v: v.tensor_scalar(
                             rr[0:1, 512:1024], pAV2[64:65, 0:512], EPS_SH, None, ALU.add),
                             i=[("dve", 1)])
                        em.o("dve", lambda v: v.reciprocal(rb[0:1, :], rr[0:1, :]),
                             w=[("dve", em.c("dve"))], i=[("dve", 1)])
                        mrec = em.c("dve")
                        bspar = step % 3
                        em.o("pe", lambda t, o=pS[bspar][0:64, 0:512]:
                             t.matmul(o, onesb[0:1, 0:64], rb[0:1, 0:512],
                             start=True, stop=True, tile_position=(0, 0)),
                             w=[("dve", mrec), ("act", mk_exp[bspar])])
                        em.o("pe", lambda t, o=pS[bspar][64:128, 0:512]:
                             t.matmul(o, onesb[0:1, 0:64], rb[0:1, 512:1024],
                             start=True, stop=True, tile_position=(0, 64)), i=[("pe", 1)])
                        mbc = em.c("pe")
                        em.o("act", lambda s, a=pS[bspar][:, 0:512]:
                             s.activation(rbig[:, :], a, AF.Copy),
                             w=[("pe", mbc), ("dve", mk_rbTT)], i=[("act", 1)])
                        mk_rb = em.c("act")
                        mk_exp[bspar] = mk_rb
                        apar = qb % 2
                        em.o("dve", lambda v, o=aTst[apar][0:64, :]:
                             v.tensor_tensor(o, pAV1[0:64, :], rbig[0:64, :], ALU.mult),
                             w=[("act", mk_rb), ("dx", mk_aSt[apar])])
                        em.o("dve", lambda v, o=aTst[apar][64:128, :]:
                             v.tensor_tensor(o, pAV2[0:64, :], rbig[64:128, :], ALU.mult),
                             i=[("dve", 1)])
                        mk_aTT[apar] = em.c("dve")
                        mk_rbTT = em.c("dve")
                        em.o("sp", lambda s, o=dram["aT_s"][p][:, qb * 512:(qb + 1) * 512],
                             src=aTst[apar]: s.dma_start(out=o, in_=src[:, :]),
                             w=[("dve", mk_aTT[apar])], i=[("dx", 16)])
                        mk_aSt[apar] = em.c("dx")
                em.flush()

            em.phase(tag + "3")
            ph = contextlib.ExitStack()
            with ph:
                T2 = T // 2
                TI2 = TI // 2
                aT_h = ph.enter_context(nc.sbuf_tensor(f"{tag}ah", [128, EC * T2], BF16))
                gw_sb = ph.enter_context(nc.sbuf_tensor(f"{tag}gw", [128, EC * NE], BF16))
                gb_sb = ph.enter_context(nc.sbuf_tensor(f"{tag}gb", [1, NE], BF16))
                we_sb = [ph.enter_context(nc.sbuf_tensor(f"{tag}we{j}", [128, EC * E], BF16)) for j in range(2)]
                web_sb = ph.enter_context(nc.sbuf_tensor(f"{tag}web", [1, NE * E], BF16))
                fw_sb = ph.enter_context(nc.sbuf_tensor(f"{tag}fw", [128, EC * E], BF16))
                fb_sb = ph.enter_context(nc.sbuf_tensor(f"{tag}fb", [1, E], BF16))
                moe = ph.enter_context(nc.sbuf_tensor(f"{tag}moe", [128, TI2 * E], F32))
                msel = ph.enter_context(nc.sbuf_tensor(f"{tag}ms", [128, TI2 * NE], F32))
                gtmp = ph.enter_context(nc.sbuf_tensor(f"{tag}gt", [128, 4 * NE], F32))
                gstat = ph.enter_context(nc.sbuf_tensor(f"{tag}gs", [128, 6], F32))
                moeb = ph.enter_context(nc.sbuf_tensor(f"{tag}mb", [128, E], BF16))
                moeT = ph.enter_context(nc.sbuf_tensor(f"{tag}mT", [128, E], BF16))
                xs3 = [ph.enter_context(nc.sbuf_tensor(f"{tag}x3{j}", [128, E], F32)) for j in range(2)]
                sqs3 = ph.enter_context(nc.sbuf_tensor(f"{tag}sq3", [128, E], F32))
                st3 = ph.enter_context(nc.sbuf_tensor(f"{tag}st3", [128, 4], F32))
                y05 = ph.enter_context(nc.sbuf_tensor(f"{tag}y05", [128, E], F32))
                fout = ph.enter_context(nc.sbuf_tensor(f"{tag}fo", [128, E], F32))
                pG = ph.enter_context(nc.psum_tensor(f"{tag}pG", [128, NE], F32))
                pE_ = [ph.enter_context(nc.psum_tensor(f"{tag}pE{j}", [128, E], F32)) for j in range(2)]
                ptr3 = ph.enter_context(nc.psum_tensor(f"{tag}p3", [128, E], BF16))
                pF = ph.enter_context(nc.psum_tensor(f"{tag}pF", [128, E], F32))

                em.begin()
                wload("fgw", gw_sb, EC, NE)
                em.o("gp", lambda g: g.dma_start(out=gb_sb[:, :], in_=dram["fgb"][:, :]), i=[("dg", 16)])
                em.o("gp", lambda g: g.dma_start(
                     out=web_sb[:, :].rearrange("p (n f) -> p n f", n=NE), in_=dram["feb"][None, :, :]),
                     i=[("dg", 16)])
                wload("ffw", fw_sb, EC, E)
                em.o("gp", lambda g: g.dma_start(out=fb_sb[:, :], in_=dram["ffb"][:, :]), i=[("dg", 16)])
                mk_w3 = em.c("dg")

                for half in range(2):
                    t0 = half * TI2
                    em.o("sp", lambda s, o=aT_h[:, :].rearrange("p (c t) -> p c t", c=EC),
                         src=dram["aT_s"][:, :, half * T2:(half + 1) * T2].transpose([1, 0, 2]):
                         s.dma_start(out=o, in_=src), w=[("pe", em.c("pe"))], i=[("dx", 16)])
                    mk_ah = em.c("dx")
                    em.phase(tag + "3g")
                    mk_pgf = 0
                    for tt in range(TI2):
                        for c in range(EC):
                            em.o("pe", lambda t, l=aT_h[:, c * T2 + tt * 128:c * T2 + (tt + 1) * 128],
                                 r=gw_sb[:, c * NE:(c + 1) * NE], st=(c == 0):
                                 t.matmul(pG[:, :], l, r, start=st, stop=False),
                                 w=([("dx", mk_ah), ("dg", mk_w3), ("act", mk_pgf)] if c == 0 else ()))
                        em.o("pe", lambda t: t.matmul(pG[:, :], onesb[0:1, 0:128], gb_sb[0:1, :],
                             start=False, stop=True), i=[("pe", 1)])
                        mg = em.c("pe")
                        gmx, gneg, gsum, grec, m1, m2 = (gstat[:, j:j + 1] for j in range(6))
                        ge, gh, g2, gm = (gtmp[:, j * NE:(j + 1) * NE] for j in range(4))
                        def dv(fn, w=()):
                            em.o("dve", fn, w=[("dve", em.c("dve"))] + list(w), i=[("dve", 1)])

                        dv(lambda v: v.reduce_max(gmx, pG[:, :], axis=AX.X), w=[("pe", mg)])
                        dv(lambda v: v.tensor_scalar(gneg, gmx, -1.0, None, ALU.mult))
                        md = em.c("dve")
                        em.o("act", lambda s: s.activation(ge, pG[:, :], AF.Exp, bias=gneg),
                             w=[("dve", md)], i=[("act", 1)])
                        mk_pgf = em.c("act")
                        dv(lambda v: v.reduce_sum(gsum, ge, axis=AX.X), w=[("act", mk_pgf)])
                        dv(lambda v: v.reciprocal(grec, gsum))
                        dv(lambda v: v.tensor_scalar(gh, ge, grec, None, ALU.mult))
                        dv(lambda v: v.reduce_max(m1, gh, axis=AX.X))
                        dv(lambda v: v.tensor_scalar(gm, gh, m1, None, ALU.is_ge))
                        dv(lambda v: v.scalar_tensor_tensor(g2, gm, -1e30, gh, ALU.mult, ALU.add))
                        dv(lambda v: v.reduce_max(m2, g2, axis=AX.X))
                        dv(lambda v: v.tensor_scalar(g2, g2, m2, None, ALU.is_ge))
                        dv(lambda v: v.tensor_tensor(gm, gm, g2, ALU.add))
                        dv(lambda v, o=msel[:, tt * NE:(tt + 1) * NE]:
                           v.tensor_tensor(o, gm, gh, ALU.mult))
                    mk_gate = em.c("dve")
                    em.phase(tag + "3e")
                    em.o("gp", lambda g, o=we_sb[0],
                         s=dram["few"][0].rearrange("(c p) f -> p c f", p=128):
                         g.dma_start(out=o[:, :].rearrange("p (c f) -> p c f", c=EC), in_=s),
                         w=[("pe", em.c("pe"))], i=[("dg", 16)])
                    mk_we = [em.c("dg"), 0]
                    mk_moe = [mk_gate] * TI2
                    mk_peu = [0, 0]
                    for n in range(NE):
                        npar = n % 2
                        if n + 1 < NE:
                            em.o("gp", lambda g, o=we_sb[1 - npar],
                                 s=dram["few"][n + 1].rearrange("(c p) f -> p c f", p=128):
                                 g.dma_start(out=o[:, :].rearrange("p (c f) -> p c f", c=EC), in_=s),
                                 w=[("pe", mk_peu[1 - npar])], i=[("dg", 16)])
                            mk_we[1 - npar] = em.c("dg")
                        for tt in range(TI2):
                            epar = tt % 2
                            for hh in range(2):
                                w0 = hh * 512
                                for c in range(EC):
                                    ww = ([("dg", mk_we[npar]), ("dve", mk_moe[tt])]
                                          if (hh == 0 and c == 0) else ())
                                    em.o("pe", lambda t, o=pE_[epar][:, w0:w0 + 512],
                                         l=aT_h[:, c * T2 + tt * 128:c * T2 + (tt + 1) * 128],
                                         r=we_sb[npar][:, c * E + w0:c * E + w0 + 512], st=(c == 0):
                                         t.matmul(o, l, r, start=st, stop=False), w=ww)
                                em.o("pe", lambda t, o=pE_[epar][:, w0:w0 + 512],
                                     r=web_sb[0:1, n * E + w0:n * E + w0 + 512]:
                                     t.matmul(o, onesb[0:1, 0:128], r, start=False, stop=True),
                                     i=([("pe", 1)] if hh == 1 else []))
                            mk_peu[epar] = em.c("pe")
                            mpe = em.c("pe")
                            if n == 0:
                                em.o("dve", lambda v, o=moe[:, tt * E:(tt + 1) * E],
                                     a=pE_[epar], s1=msel[:, tt * NE + n:tt * NE + n + 1]:
                                     v.tensor_scalar(o, a[:, :], s1, None, ALU.mult),
                                     w=[("pe", mpe)], i=[("dve", 1)])
                            else:
                                em.o("dve", lambda v, o=moe[:, tt * E:(tt + 1) * E],
                                     a=pE_[epar], s1=msel[:, tt * NE + n:tt * NE + n + 1]:
                                     v.scalar_tensor_tensor(o, a[:, :], s1, o, ALU.mult, ALU.add),
                                     w=[("pe", mpe)], i=[("dve", 1)])
                            mk_moe[tt] = em.c("dve")
                    em.phase(tag + "3f")
                    mk_fin = [0, 0]
                    mk_tr3 = 0
                    mk_mTu = 0
                    mk_fst = 0
                    for tt in range(TI2):
                        ig = t0 + tt
                        par = tt % 2
                        em.o("sp", lambda s, o=xs3[par], src=xin_ap,
                             sl=slice(ig * 128, (ig + 1) * 128): s.dma_start(out=o[:, :], in_=src[sl, :]),
                             w=[("dve", mk_fin[par])], i=[("dx", 16)])
                        mk_xl = em.c("dx")
                        em.o("dve", lambda v, a=moe[:, tt * E:(tt + 1) * E]:
                             v.tensor_copy(moeb[:, :], a), w=[("dve", mk_moe[tt]), ("pe", mk_tr3)],
                             i=[("dve", 1)])
                        mcast = em.c("dve")
                        pem, dvem = transpose_n(moeb, ptr3, moeT, 8, pe_w=[("dve", mcast)],
                                                dve_w=[("pe", mk_mTu)])
                        mk_tr3 = pem
                        pe_f = proj_mm(pF, moeT, fw_sb, E, fb_sb,
                                       pe_w=[("dve", dvem), ("dve", max(mk_fin))])
                        mk_mTu = pe_f
                        actm = ln_chain(xs3[par], st3, sqs3, LN_HALF, [("dx", mk_xl)])
                        em.o("dve", lambda v, a=xs3[par]: v.tensor_scalar(
                             y05[:, :], a[:, :], st3[:, 0:1], st3[:, 3:4], ALU.subtract, ALU.mult),
                             w=[("act", actm)], i=[("dve", 1)])
                        em.o("dve", lambda v: v.scalar_tensor_tensor(
                             fout[:, :], pF[:, :], 0.5, y05[:, :], ALU.mult, ALU.add),
                             w=[("pe", pe_f), ("dx", mk_fst)])
                        em.o("dve", lambda v, a=xs3[par]: v.tensor_tensor(
                             fout[:, :], fout[:, :], a[:, :], ALU.add), i=[("dve", 1)])
                        mk_fin[par] = em.c("dve")
                        em.o("sp", lambda s, o=xout_ap, src=fout,
                             sl=slice(ig * 128, (ig + 1) * 128): s.dma_start(out=o[sl, :], in_=src[:, :]),
                             w=[("dve", mk_fin[par])], i=[("dx", 16)])
                        mk_fst = em.c("dx")
                em.flush()

        def mixer_call(xin_ap, xout_ap):
            em.phase("m0")
            ph = contextlib.ExitStack()
            with ph:
                w_in = ph.enter_context(nc.sbuf_tensor("mwin_sb", [128, EC * 2 * DEXP], BF16))
                wq_sb = ph.enter_context(nc.sbuf_tensor("mwq_sb", [128, EC * DEXP], BF16))
                wk_sb = ph.enter_context(nc.sbuf_tensor("mwk_sb", [128, EC * DEXP], BF16))
                bin_bf = ph.enter_context(nc.sbuf_tensor("mbinb", [1, 2 * DEXP], BF16))
                bq_bf = ph.enter_context(nc.sbuf_tensor("mbqb", [1, DEXP], BF16))
                bk_bf = ph.enter_context(nc.sbuf_tensor("mbkb", [1, DEXP], BF16))
                sgr = ph.enter_context(nc.sbuf_tensor("msg", [128, 48], F32))

                fs = contextlib.ExitStack()
                with fs:
                    lnv = {}
                    for n in ["inlnw", "inlnb", "qlnw", "qlnb", "klnw", "klnb"]:
                        lnv[n] = fs.enter_context(nc.sbuf_tensor(f"m_{n}", [128, EC], F32))
                    lnb_bf = {}
                    for n in ["inlnb", "qlnb", "klnb"]:
                        lnb_bf[n] = fs.enter_context(nc.sbuf_tensor(f"mbf_{n}", [128, EC], BF16))
                    bacc = fs.enter_context(nc.sbuf_tensor("mbacc", [1, 2 * DEXP], F32))
                    wtmp = fs.enter_context(nc.sbuf_tensor("mwt", [128, 2 * DEXP], F32))
                    wrawb = fs.enter_context(nc.sbuf_tensor("mwrb", [128, 2 * DEXP], BF16))
                    srow = fs.enter_context(nc.sbuf_tensor("msr", [1, 16], F32))
                    pbr = fs.enter_context(nc.psum_tensor("mpbr", [1, 512], F32))
                    pbx = fs.enter_context(nc.psum_tensor("mpbx", [128, 512], F32))

                    em.begin()
                    for n in lnv:
                        em.o("sp", lambda s, o=lnv[n], nm=n: s.dma_start(out=o[:, :], in_=dram[nm][:, :]),
                             i=[("dx", 16)])
                    em.o("sp", lambda s: s.dma_start(out=srow[:, :], in_=dram["smear"][:, :]), i=[("dx", 16)])
                    mk_lnv = em.c("dx")
                    em.o("act", lambda s: s.activation(sgr[0:1, 0:16], srow[0:1, :], AF.Sigmoid),
                         w=[("dx", mk_lnv)], i=[("act", 1)])
                    em.o("sp", lambda s: s.dma_start(out=srow[:, :], in_=dram["logscale"][:, :]),
                         w=[("act", em.c("act"))], i=[("dx", 16)])
                    em.o("act", lambda s: s.activation(sgr[0:1, 32:48], srow[0:1, :], AF.Exp, scale=-1.0),
                         w=[("dx", em.c("dx"))], i=[("act", 1)])
                    mk_a2 = em.c("act")
                    em.o("dve", lambda v: v.tensor_scalar(sgr[0:1, 16:32], sgr[0:1, 0:16], -1.0, 1.0,
                         ALU.mult, ALU.add), w=[("act", mk_a2)], i=[("dve", 1)])
                    mk_sg = em.c("dve")
                    em.o("pe", lambda t: t.matmul(pbx[:, 0:48], onesf[0:1, :], sgr[0:1, 0:48],
                         start=True, stop=True), w=[("dve", mk_sg)], i=[("pe", 1)])
                    em.o("dve", lambda v: v.tensor_copy(sgr[:, 0:48], pbx[:, 0:48]),
                         w=[("pe", em.c("pe"))], i=[("dve", 1)])
                    for n in ["inlnb", "qlnb", "klnb"]:
                        em.o("dve", lambda v, o=lnb_bf[n], s=lnv[n]: v.tensor_copy(o[:, :], s[:, :]),
                             w=[("dx", mk_lnv)], i=[("dve", 1)])

                    def fold2(src, wdst, wcols, lnw, lnbname, bias_src, bf_out):
                        if bias_src is not None:
                            em.o("sp", lambda s, o=bacc[0:1, 0:wcols]:
                                 s.dma_start(out=o, in_=dram[bias_src][:, :]),
                                 w=[("dve", em.c("dve"))], i=[("dx", 16)])
                            mk_b0 = [("dx", em.c("dx"))]
                        else:
                            em.o("dve", lambda v: v.memset(bacc[0:1, 0:wcols], 0.0),
                                 i=[("dve", 1)])
                            mk_b0 = [("dve", em.c("dve"))]
                        mk_bprev = [("dve", 0)]
                        for c in range(EC):
                            em.o("gp", lambda g, o=wtmp[:, 0:wcols],
                                 s=dram[src][c * 128:(c + 1) * 128, :]: g.dma_start(out=o, in_=s),
                                 w=[("dve", em.c("dve")), ("pe", em.c("pe"))], i=[("dg", 16)])
                            mw = em.c("dg")
                            em.o("dve", lambda v, wc=wcols: v.tensor_copy(
                                 wrawb[:, 0:wc], wtmp[:, 0:wc]),
                                 w=[("dg", mw), ("dx", mk_lnv), ("pe", em.c("pe"))], i=[("dve", 1)])
                            mraw = em.c("dve")
                            em.o("dve", lambda v, o=wdst[:, c * wcols:(c + 1) * wcols],
                                 s1=lnv[lnw][:, c:c + 1], wc=wcols:
                                 v.tensor_scalar(o, wtmp[:, 0:wc], s1, None, ALU.mult),
                                 i=[("dve", 1)])
                            for sl in range(wcols // 512):
                                s0 = sl * 512
                                em.o("pe", lambda t, l=lnb_bf[lnbname][:, c:c + 1],
                                     r=wrawb[:, s0:s0 + 512]: t.matmul(
                                     pbr[0:1, 0:512], l, r, start=True, stop=True),
                                     w=[("dve", mraw)] + mk_b0 + mk_bprev, i=[("pe", 1)])
                                mpb = em.c("pe")
                                em.o("dve", lambda v, o=bacc[0:1, s0:s0 + 512]:
                                     v.tensor_tensor(o, pbr[0:1, 0:512], o, ALU.add),
                                     w=[("pe", mpb)], i=[("dve", 1)])
                                mk_bprev = [("dve", em.c("dve"))]
                        em.o("dve", lambda v, wc=wcols, o=bf_out: v.tensor_copy(
                             o[0:1, 0:wc], bacc[0:1, 0:wc]), i=[("dve", 1)])
                        return em.c("dve")

                    fold2("mwin", w_in, 2 * DEXP, "inlnw", "inlnb", None, bin_bf)
                    fold2("mwq", wq_sb, DEXP, "qlnw", "qlnb", "mbq", bq_bf)
                    fold2("mwk", wk_sb, DEXP, "klnw", "klnb", "mbk", bk_bf)
                    em.flush()

                em.phase("mp")
                ps = contextlib.ExitStack()
                with ps:
                    xz = [ps.enter_context(nc.sbuf_tensor(f"mxz{j}", [128, E], F32)) for j in range(2)]
                    y1 = ps.enter_context(nc.sbuf_tensor("my1", [128, E], F32))
                    z_bf = ps.enter_context(nc.sbuf_tensor("mzb", [128, E], BF16))
                    zT = [ps.enter_context(nc.sbuf_tensor(f"mzT{j}", [128, E], BF16)) for j in range(2)]
                    sqz = ps.enter_context(nc.sbuf_tensor("msqz", [128, E], F32))
                    stz = ps.enter_context(nc.sbuf_tensor("mstz", [128, 4], F32))
                    qe = ps.enter_context(nc.sbuf_tensor("mqe", [128, DEXP], F32))
                    hsum = ps.enter_context(nc.sbuf_tensor("mhs", [128, 32], F32))
                    qt_bf = ps.enter_context(nc.sbuf_tensor("mqt", [128, DEXP], BF16))
                    kt_bf = [ps.enter_context(nc.sbuf_tensor(f"mkt{j}", [128, DEXP], BF16)) for j in range(2)]
                    ksh = ps.enter_context(nc.sbuf_tensor("mksh", [128, DEXP], BF16))
                    ktmp = ps.enter_context(nc.sbuf_tensor("mktm", [128, DEXP], BF16))
                    ksm = ps.enter_context(nc.sbuf_tensor("mksm", [128, DEXP], BF16))
                    tbig = ps.enter_context(nc.sbuf_tensor("mtb", [128, DEXP], BF16))
                    pqk = ps.enter_context(nc.psum_tensor("mpqk", [128, DEXP], F32))
                    ptrm = ps.enter_context(nc.psum_tensor("mptr", [128, E], BF16))

                    em.begin()
                    mk = defaultdict(int)
                    mk_bb = 0
                    mk_sgr = 0
                    for c in range(TI):
                        par = c % 2
                        em.o("sp", lambda s, o=xz[par], src=xin_ap, sl=slice(c * 128, (c + 1) * 128):
                             s.dma_start(out=o[:, :], in_=src[sl, :]), w=[("dve", mk[f"xzf{par}"])],
                             i=[("dx", 16)])
                        mk_x = em.c("dx")
                        actm = ln_chain(xz[par], stz, sqz, 0.0, [("dx", mk_x)])
                        em.o("dve", lambda v, a=xz[par]: v.tensor_scalar(
                             y1[:, :], a[:, :], stz[:, 0:1], stz[:, 3:4], ALU.subtract, ALU.mult),
                             w=[("act", actm)], i=[("dve", 1)])
                        mk[f"xzf{par}"] = em.c("dve")
                        actm = ln_chain(y1, stz, sqz, 0.0, [("dve", em.c("dve"))])
                        em.o("dve", lambda v: v.tensor_scalar(
                             z_bf[:, :], y1[:, :], stz[:, 0:1], stz[:, 3:4], ALU.subtract, ALU.mult),
                             w=[("act", actm), ("pe", mk["ztr"])], i=[("dve", 1)])
                        mzb = em.c("dve")
                        pem, dvem = transpose_n(z_bf, ptrm, zT[par], 8, pe_w=[("dve", mzb)],
                                                dve_w=[("pe", mk["zT_busy"])])
                        mk["ztr"] = pem
                        mk_zT = dvem

                        def mproj(wsb, wcols, bias_bf, pe_w):
                            for hh in range(wcols // 512):
                                w0 = hh * 512
                                for cc in range(EC):
                                    em.o("pe", lambda t, o=pqk[:, w0:w0 + 512],
                                         l=zT[par][:, cc * 128:(cc + 1) * 128],
                                         r=wsb[:, cc * wcols + w0:cc * wcols + w0 + 512], st=(cc == 0):
                                         t.matmul(o, l, r, start=st, stop=False),
                                         w=(list(pe_w) if (hh == 0 and cc == 0) else ()))
                                em.o("pe", lambda t, o=pqk[:, w0:w0 + 512], r=bias_bf[0:1, w0:w0 + 512]:
                                     t.matmul(o, onesb[0:1, 0:128], r, start=False, stop=True),
                                     i=([("pe", 1)] if hh == wcols // 512 - 1 else []))
                            return em.c("pe")

                        def softdiv(dst_bf, extra_w=()):
                            em.o("dve", lambda v: v.reduce_sum(
                                 hsum[:, 0:16], qe[:, :].rearrange("p (h d) -> p h d", h=16), axis=AX.X),
                                 w=list(extra_w), i=[("dve", 1)])
                            em.o("dve", lambda v: v.reciprocal(hsum[:, 0:16], hsum[:, 0:16]),
                                 w=[("dve", em.c("dve"))], i=[("dve", 1)])
                            em.o("dve", lambda v: v.tensor_tensor(
                                 hsum[:, 16:32], hsum[:, 0:16], sgr[:, 32:48], ALU.mult),
                                 w=[("dve", em.c("dve"))], i=[("dve", 1)])
                            em.o("dve", lambda v, o=dst_bf: v.tensor_tensor(
                                 o[:, :].rearrange("p (h d) -> p h d", h=16),
                                 qe[:, :].rearrange("p (h d) -> p h d", h=16),
                                 hsum[:, 16:32][:, :, None].broadcast_to([128, 16, 128]), ALU.mult),
                                 w=[("dve", em.c("dve"))], i=[("dve", 1)])
                            return em.c("dve")

                        mq = mproj(wq_sb, DEXP, bq_bf,
                                   [("dve", mk_zT), ("dve", mk_bb), ("act", mk["qe_act"]),
                                    ("dve", mk["qe_dve"])])
                        em.o("act", lambda s: s.activation(qe[:, :], pqk[:, :], AF.Exp),
                             w=[("pe", mq), ("dve", mk["qe_dve"])], i=[("act", 1)])
                        mk["qe_act"] = em.c("act")
                        mdq = softdiv(qt_bf, extra_w=[("act", mk["qe_act"]), ("pe", mk["qt_busy"]),
                                                      ("dve", mk_sgr)])
                        mk["qe_dve"] = mdq
                        pem, dvem = transpose16(qt_bf, ptrm, tbig, pe_w=[("dve", mdq)],
                                                dve_w=[("dx", mk["tbig_store"])])
                        mk["qt_busy"] = pem
                        mk["zT_busy"] = pem
                        em.o("sp", lambda s, o=dram["mqhT"][c], src=tbig:
                             s.dma_start(out=o[:, :], in_=src[:, :]), w=[("dve", dvem)], i=[("dx", 16)])
                        mk["st_q"] = em.c("dx")
                        mk["tbig_store"] = em.c("dx")
                        mkp = mproj(wk_sb, DEXP, bk_bf, [("act", mk["qe_act"]), ("dve", mk["qe_dve"])])
                        em.o("act", lambda s: s.activation(qe[:, :], pqk[:, :], AF.Exp),
                             w=[("pe", mkp), ("dve", mk["qe_dve"])], i=[("act", 1)])
                        mk["qe_act"] = em.c("act")
                        mdk = softdiv(kt_bf[par], extra_w=[("act", mk["qe_act"]), ("dg", mk["ksh_gp"]),
                                                           ("dve", mk["ksm_dve"])])
                        mk["qe_dve"] = mdk
                        em.o("gp", lambda g, o=ksh[1:128, :], s=kt_bf[par][0:127, :]:
                             g.dma_start(out=o, in_=s), w=[("dve", mdk), ("dve", mk["ksh_dve"])],
                             i=[("dg", 16)])
                        if c == 0:
                            em.o("dve", lambda v: v.memset(ksh[0:1, :], 0.0), i=[("dve", 1)])
                            mk_top = ("dve", em.c("dve"))
                        else:
                            em.o("gp", lambda g, o=ksh[0:1, :], s=kt_bf[1 - par][127:128, :]:
                                 g.dma_start(out=o, in_=s), i=[("dg", 16)])
                            mk_top = ("dg", em.c("dg"))
                        mk["ksh_gp"] = em.c("dg")
                        em.o("dve", lambda v: v.tensor_tensor(
                             ktmp[:, :].rearrange("p (h d) -> p h d", h=16),
                             ksh[:, :].rearrange("p (h d) -> p h d", h=16),
                             sgr[:, 0:16][:, :, None].broadcast_to([128, 16, 128]), ALU.mult),
                             w=[("dg", mk["ksh_gp"]), mk_top, ("dx", mk["st_ktl"])])
                        em.o("dve", lambda v, a=kt_bf[par][:, :].rearrange("p (h d) -> p h d", h=16):
                             v.tensor_tensor(
                             ksm[:, :].rearrange("p (h d) -> p h d", h=16), a,
                             sgr[:, 16:32][:, :, None].broadcast_to([128, 16, 128]), ALU.mult),
                             w=[("pe", mk["ksm_busy"])])
                        em.o("dve", lambda v: v.tensor_tensor(ksm[:, :], ksm[:, :], ktmp[:, :], ALU.add),
                             i=[("dve", 1)])
                        mk["ksh_dve"] = em.c("dve")
                        msm = em.c("dve")
                        em.o("dve", lambda v: v.tensor_tensor(
                             ktmp[:, :].rearrange("p (h d) -> p h d", h=16),
                             ksm[:, :].rearrange("p (h d) -> p h d", h=16),
                             dram_lamk_bc, ALU.mult), w=[("dx", mk["st_ktl"])], i=[("dve", 1)])
                        mktl = em.c("dve")
                        em.o("sp", lambda s, o=dram["mktl"][c], src=ktmp:
                             s.dma_start(out=o[:, :], in_=src[:, :]), w=[("dve", mktl)], i=[("dx", 16)])
                        mk["st_ktl"] = em.c("dx")
                        mk["tbig_store"] = em.c("dx")
                        mk["ksm_dve"] = mktl
                        pem, dvem = transpose16(ksm, ptrm, tbig,
                                                pe_w=[("dve", msm)],
                                                dve_w=[("dx", mk["tbig_store"])])
                        mk["ksm_busy"] = pem
                        em.o("sp", lambda s, o=dram["mkhT"][c], src=tbig:
                             s.dma_start(out=o[:, :], in_=src[:, :]), w=[("dve", dvem)], i=[("dx", 16)])
                        mk["st_k"] = em.c("dx")
                        mk["tbig_store"] = em.c("dx")
                        def mproj_vp(off, pe_w):
                            for hh in range(4):
                                w0 = hh * 512
                                for cc in range(EC):
                                    em.o("pe", lambda t, o=pqk[:, w0:w0 + 512],
                                         l=zT[par][:, cc * 128:(cc + 1) * 128],
                                         r=w_in[:, cc * 2 * DEXP + off + w0:cc * 2 * DEXP + off + w0 + 512],
                                         st=(cc == 0): t.matmul(o, l, r, start=st, stop=False),
                                         w=(list(pe_w) if (hh == 0 and cc == 0) else ()))
                                em.o("pe", lambda t, o=pqk[:, w0:w0 + 512],
                                     r=bin_bf[0:1, off + w0:off + w0 + 512]:
                                     t.matmul(o, onesb[0:1, 0:128], r, start=False, stop=True),
                                     i=([("pe", 1)] if hh == 3 else []))
                            return em.c("pe")

                        mv = mproj_vp(0, [("act", mk["qe_act"]), ("dve", mk["qe_dve"])])
                        em.o("dve", lambda v: v.tensor_copy(tbig[:, :], pqk[:, :]),
                             w=[("pe", mv), ("dx", mk["tbig_store"])], i=[("dve", 1)])
                        mvv = em.c("dve")
                        em.o("sp", lambda s, o=dram["mvtk"][c], src=tbig:
                             s.dma_start(out=o[:, :], in_=src[:, :]), w=[("dve", mvv)], i=[("dx", 16)])
                        mk["st_v"] = em.c("dx")
                        mk["tbig_store"] = em.c("dx")
                        mp = mproj_vp(DEXP, [("dve", mvv)])
                        em.o("act", lambda s: s.activation(qt_bf[:, :], pqk[:, :], AF.Silu),
                             w=[("pe", mp), ("pe", mk["qt_busy"]), ("dve", 0)], i=[("act", 1)])
                        mps = em.c("act")
                        pem, dvem = transpose16(qt_bf, ptrm, tbig,
                                                pe_w=[("act", mps)],
                                                dve_w=[("dx", mk["tbig_store"])])
                        mk["qt_busy"] = pem
                        em.o("sp", lambda s, o=dram["mphT"][c], src=tbig:
                             s.dma_start(out=o[:, :], in_=src[:, :]), w=[("dve", dvem)], i=[("dx", 16)])
                        mk["st_p"] = em.c("dx")
                        mk["tbig_store"] = em.c("dx")
                    em.flush()

            em.phase("ms")
            ph = contextlib.ExitStack()
            with ph:
                wo_sb = ph.enter_context(nc.sbuf_tensor("mwo", [128, HF * E], BF16))
                maskT_sb = ph.enter_context(nc.sbuf_tensor("mmask", [128, 16 * 128], F32))
                lamqf_sb = ph.enter_context(nc.sbuf_tensor("mlqf", [128, 16 * 128], BF16))
                olnw_bc = ph.enter_context(nc.sbuf_tensor("molw", [128, E], F32))
                olnb_bc = ph.enter_context(nc.sbuf_tensor("molb", [128, E], F32))
                orow = ph.enter_context(nc.sbuf_tensor("morow", [1, E], F32))
                qhT = [ph.enter_context(nc.sbuf_tensor(f"sqh{j}", [128, DEXP], BF16)) for j in range(2)]
                qlT = ph.enter_context(nc.sbuf_tensor("sql", [128, DEXP], BF16))
                khT = [ph.enter_context(nc.sbuf_tensor(f"skh{j}", [128, DEXP], BF16)) for j in range(2)]
                ktl = [ph.enter_context(nc.sbuf_tensor(f"skt{j}", [128, DEXP], BF16)) for j in range(2)]
                v_tok = [ph.enter_context(nc.sbuf_tensor(f"svt{j}", [128, DEXP], BF16)) for j in range(2)]
                phT = [ph.enter_context(nc.sbuf_tensor(f"sph{j}", [128, DEXP], BF16)) for j in range(2)]
                state_f = ph.enter_context(nc.sbuf_tensor("sstf", [128, 16 * 132], F32))
                state_b = ph.enter_context(nc.sbuf_tensor("sstb", [128, 16 * 132], BF16))
                smb = ph.enter_context(nc.sbuf_tensor("ssmb", [128, 128], BF16))
                rb2 = ph.enter_context(nc.sbuf_tensor("srb2", [128, 128], F32))
                otm = ph.enter_context(nc.sbuf_tensor("sotm", [128, 128], F32))
                oT = ph.enter_context(nc.sbuf_tensor("soT", [128, DEXP], BF16))
                o2 = ph.enter_context(nc.sbuf_tensor("so2", [128, E], F32))
                sqo = ph.enter_context(nc.sbuf_tensor("ssqo", [128, E], F32))
                sto = ph.enter_context(nc.sbuf_tensor("ssto", [128, 4], F32))
                xr = [ph.enter_context(nc.sbuf_tensor(f"sxr{j}", [128, E], F32)) for j in range(2)]
                xo = ph.enter_context(nc.sbuf_tensor("sxo", [128, E], F32))
                dtm = ph.enter_context(nc.sbuf_tensor("sdtm", [1, 128], F32))
                rden = ph.enter_context(nc.sbuf_tensor("srdn", [1, 128], BF16))
                pSl = ph.enter_context(nc.psum_tensor("spSl", [128, 128], F32))
                pOD = ph.enter_context(nc.psum_tensor("spOD", [128, 256], F32))
                pSt = ph.enter_context(nc.psum_tensor("spSt", [128, 132], F32))
                pWo = ph.enter_context(nc.psum_tensor("spWo", [128, E], F32))
                pBx = ph.enter_context(nc.psum_tensor("spBx", [128, 512], F32))

                em.begin()
                em.o("gp", lambda g, s=dram["mwout"].rearrange("(c p) f -> p c f", p=128):
                     g.dma_start(out=wo_sb[:, :].rearrange("p (c f) -> p c f", c=HF), in_=s),
                     i=[("dg", 16)])
                em.o("gp", lambda g: g.dma_start(out=maskT_sb[:, :], in_=dram["maskT"][:, :]),
                     i=[("dg", 16)])
                em.o("gp", lambda g: g.dma_start(out=lamqf_sb[:, :], in_=dram["lamq_feat"][:, :]),
                     i=[("dg", 16)])
                mk_cst = em.c("dg")
                em.o("sp", lambda s: s.dma_start(out=orow[:, :], in_=dram["outlnw"][:, :]),
                     i=[("dx", 16)])
                mk_or = em.c("dx")
                for hh, dst in [(0, olnw_bc), (1, olnw_bc)]:
                    pass
                for (row, dst) in [("outlnw", olnw_bc), ("outlnb", olnb_bc)]:
                    em.o("sp", lambda s, nm=row: s.dma_start(out=orow[:, :], in_=dram[nm][:, :]),
                         w=[("dve", em.c("dve"))], i=[("dx", 16)])
                    mk_or = em.c("dx")
                    for hh in range(2):
                        em.o("pe", lambda t, r=orow[0:1, hh * 512:(hh + 1) * 512]:
                             t.matmul(pBx[:, :], onesf[0:1, :], r, start=True, stop=True),
                             w=[("dx", mk_or), ("dve", em.c("dve"))], i=[("pe", 1)])
                        em.o("dve", lambda v, o=dst[:, hh * 512:(hh + 1) * 512]:
                             v.tensor_copy(o, pBx[:, :]), w=[("pe", em.c("pe"))], i=[("dve", 1)])
                em.o("dve", lambda v: v.memset(state_f[:, :], 0.0))
                em.o("dve", lambda v: v.memset(state_b[:, :], 0.0), i=[("dve", 1)])
                mk_st0 = em.c("dve")

                mk = defaultdict(int)
                mk_state = [mk_st0] * 16
                for c in range(TI):
                    par = c % 2
                    for nm, dst in [("mqhT", qhT[par]), ("mkhT", khT[par]), ("mktl", ktl[par]),
                                    ("mvtk", v_tok[par]), ("mphT", phT[par])]:
                        em.o("sp", lambda s, o=dst, src=dram[nm][c]:
                             s.dma_start(out=o[:, :], in_=src[:, :]),
                             w=[("pe", mk[f"busy{par}"]), ("dve", mk[f"busyd{par}"])], i=[("dx", 16)])
                    em.o("sp", lambda s, o=xr[par], src=xin_ap, sl=slice(c * 128, (c + 1) * 128):
                         s.dma_start(out=o[:, :], in_=src[sl, :]), w=[("dve", mk[f"xrf{par}"])],
                         i=[("dx", 16)])
                    mk_ld = em.c("dx")
                    em.o("dve", lambda v, a=qhT[par][:, :].rearrange("p (h d) -> p h d", h=16):
                         v.tensor_tensor(
                         qlT[:, :].rearrange("p (h d) -> p h d", h=16), a,
                         lamqf_sb[:, :].rearrange("p (h d) -> p h d", h=16), ALU.mult),
                         w=[("dx", mk_ld), ("dg", mk_cst), ("pe", mk["qlT_busy"])], i=[("dve", 1)])
                    mk_ql = em.c("dve")
                    for h in range(16):
                        em.o("pe", lambda t, l=khT[par][:, h * 128:(h + 1) * 128],
                             r=qhT[par][:, h * 128:(h + 1) * 128]:
                             t.matmul(pSl[:, :], l, r, start=True, stop=True),
                             w=[("dx", mk_ld), ("act", mk["rb2_act"]), ("dve", mk["sm_dve"])],
                             i=[("pe", 1)])
                        mk_sl = em.c("pe")
                        em.o("dve", lambda v, mh=maskT_sb[:, h * 128:(h + 1) * 128]:
                             v.tensor_tensor(smb[:, :], pSl[:, :], mh, ALU.mult),
                             w=[("pe", mk_sl)], i=[("dve", 1)])
                        mk["sm_dve"] = em.c("dve")
                        em.o("pe", lambda t, l=v_tok[par][:, h * 128:(h + 1) * 128]:
                             t.matmul(pOD[:, 0:128], l, smb[:, :], start=True, stop=False),
                             w=[("dve", mk["sm_dve"]), ("dve", mk["o_dve"])])
                        em.o("pe", lambda t, l=state_b[:, h * 132:h * 132 + 128],
                             r=qlT[:, h * 128:(h + 1) * 128]:
                             t.matmul(pOD[:, 0:128], l, r, start=False, stop=True),
                             w=[("dve", max(mk_ql, mk_state[h]))])
                        em.o("pe", lambda t: t.matmul(pOD[0:1, 128:256], onescol[:, :], smb[:, :],
                             start=True, stop=False))
                        em.o("pe", lambda t, l=state_b[:, h * 132 + 128:h * 132 + 129],
                             r=qlT[:, h * 128:(h + 1) * 128]:
                             t.matmul(pOD[0:1, 128:256], l, r, start=False, stop=True))
                        em.o("pe", lambda t, l=ktl[par][:, h * 128:(h + 1) * 128],
                             r=v_tok[par][:, h * 128:(h + 1) * 128]:
                             t.matmul(pSt[:, 0:128], l, r, start=True, stop=True),
                             w=[("dve", mk["stu_dve"])])
                        em.o("pe", lambda t, l=ktl[par][:, h * 128:(h + 1) * 128]:
                             t.matmul(pSt[:, 128:129], l, onescol[:, :], start=True, stop=True),
                             i=[("pe", 1)])
                        mk_pe_h = em.c("pe")
                        em.o("dve", lambda v, o=state_f[:, h * 132:h * 132 + 129], lc=LAMC[h]:
                             v.scalar_tensor_tensor(o, o, lc, pSt[:, 0:129], ALU.mult, ALU.add),
                             w=[("pe", mk_pe_h)])
                        em.o("dve", lambda v, o=state_b[:, h * 132:h * 132 + 129],
                             s=state_f[:, h * 132:h * 132 + 129]: v.tensor_copy(o, s),
                             i=[("dve", 1)])
                        mk_state[h] = em.c("dve")
                        mk["stu_dve"] = mk_state[h]
                        em.o("dve", lambda v: v.tensor_scalar(dtm[0:1, :], pOD[0:1, 128:256],
                             ATTN_EPS, None, ALU.add), w=[("pe", mk_pe_h)], i=[("dve", 1)])
                        em.o("dve", lambda v: v.reciprocal(rden[0:1, :], dtm[0:1, :]),
                             w=[("dve", em.c("dve"))], i=[("dve", 1)])
                        mk_rd = em.c("dve")
                        em.o("pe", lambda t: t.matmul(pSl[:, :], onesb[0:1, 0:128], rden[0:1, :],
                             start=True, stop=True), w=[("dve", mk_rd)], i=[("pe", 1)])
                        mk_bc = em.c("pe")
                        em.o("act", lambda s: s.activation(rb2[:, :], pSl[:, :], AF.Copy),
                             w=[("pe", mk_bc)], i=[("act", 1)])
                        mk["rb2_act"] = em.c("act")
                        em.o("dve", lambda v: v.tensor_tensor(otm[:, :], pOD[:, 0:128], rb2[:, :],
                             ALU.mult), w=[("act", mk["rb2_act"])])
                        em.o("dve", lambda v, o=oT[:, h * 128:(h + 1) * 128],
                             p=phT[par][:, h * 128:(h + 1) * 128]:
                             v.tensor_tensor(o, otm[:, :], p, ALU.mult), i=[("dve", 1)])
                        mk["o_dve"] = em.c("dve")
                    mk_oT = em.c("dve")
                    mk["qlT_busy"] = 0
                    for h in range(HF):
                        for hh in range(2):
                            em.o("pe", lambda t, o=pWo[:, hh * 512:(hh + 1) * 512],
                                 l=oT[:, h * 128:(h + 1) * 128],
                                 r=wo_sb[:, h * E + hh * 512:h * E + hh * 512 + 512],
                                 st=(h == 0), sp=(h == HF - 1): t.matmul(o, l, r, start=st, stop=sp),
                                 w=([("dve", mk_oT), ("dve", mk["o2_dve"]), ("dg", mk_cst)]
                                    if h == 0 and hh == 0 else ()),
                                 i=([("pe", 1)] if (h == HF - 1 and hh == 1) else []))
                    mk_wo = em.c("pe")
                    mk["qlT_busy"] = mk_wo
                    em.o("dve", lambda v: v.tensor_copy(o2[:, :], pWo[:, :]), w=[("pe", mk_wo)],
                         i=[("dve", 1)])
                    mo2 = em.c("dve")
                    actm = ln_chain(o2, sto, sqo, 0.0, [("dve", mo2)])
                    em.o("dve", lambda v: v.tensor_scalar(
                         sqo[:, :], o2[:, :], sto[:, 0:1], sto[:, 3:4], ALU.subtract, ALU.mult),
                         w=[("act", actm)])
                    em.o("dve", lambda v: v.tensor_tensor(sqo[:, :], sqo[:, :], olnw_bc[:, :],
                         ALU.mult))
                    em.o("dve", lambda v: v.tensor_tensor(sqo[:, :], sqo[:, :], olnb_bc[:, :],
                         ALU.add))
                    em.o("dve", lambda v, a=xr[par]: v.tensor_tensor(xo[:, :], sqo[:, :], a[:, :],
                         ALU.add), w=[("dx", mk["xo_st"])], i=[("dve", 1)])
                    mk[f"xrf{par}"] = em.c("dve")
                    mk["o2_dve"] = em.c("dve")
                    mxo = em.c("dve")
                    em.o("sp", lambda s, o=xout_ap, src=xo, sl=slice(c * 128, (c + 1) * 128):
                         s.dma_start(out=o[sl, :], in_=src[:, :]), w=[("dve", mxo)], i=[("dx", 16)])
                    mk["xo_st"] = em.c("dx")
                    mk[f"busy{par}"] = mk_wo
                    mk[f"busyd{par}"] = mxo
                em.flush()

        lamk_sb_holder = {}

        lamk_sb = top.enter_context(nc.sbuf_tensor("g_lamk", [128, 16], F32))
        em.begin()
        em.o("gp", lambda g: g.dma_start(out=lamk_sb[:, :], in_=dram["lamk"][:, :]), i=[("dg", 16)])
        em.flush()
        dram_lamk_bc = lamk_sb[:, :][:, :, None].broadcast_to([128, 16, 128])

        def _tap_scr(tp, srcn):
            if tp not in taps:
                return
            C = dram[srcn].shape[2]
            with nc.sbuf_tensor(f"tap{tp}", [128, C], BF16) as buf:
                em.begin()
                for cc in range(EC):
                    em.o("sp", lambda s, o=buf, si=dram[srcn][cc]:
                         s.dma_start(out=o[:, :], in_=si[:, :]),
                         w=[("dx", em.c("dx"))], i=[("dx", 16)])
                    em.o("sp", lambda s, o=dram[tp + "o"][cc], si=buf:
                         s.dma_start(out=o[:, :], in_=si[:, :]),
                         w=[("dx", em.c("dx"))], i=[("dx", 16)])
                em.o("sp", lambda s: s.nop(), w=[("dx", em.c("dx"))])
                em.flush()

        ffn_call(dram["x"], dram["x1"], "a")
        for tp, srcn in [("aqT", "qT_s"), ("akT", "kT_s"), ("av", "v_s"), ("aaT", "aT_s")]:
            _tap_scr(tp, srcn)
        if "ay" in taps:
            with nc.sbuf_tensor("tapy", [128, E], BF16) as buf:
                em.begin()
                for ti in range(TI):
                    em.o("sp", lambda s, o=buf, si=dram["y_s"][ti * 128:(ti + 1) * 128, :]:
                         s.dma_start(out=o[:, :], in_=si), w=[("dx", em.c("dx"))], i=[("dx", 16)])
                    em.o("sp", lambda s, o=dram["ayo"][ti * 128:(ti + 1) * 128, :], si=buf:
                         s.dma_start(out=o, in_=si[:, :]), w=[("dx", em.c("dx"))], i=[("dx", 16)])
                em.o("sp", lambda s: s.nop(), w=[("dx", em.c("dx"))])
                em.flush()
        if "x1" in taps:
            _copy_dram(nc, em, dram["x1"], dram["x1o"], T, "cp1")
        mixer_call(dram["x1"], dram["x2"])
        if "x2" in taps:
            _copy_dram(nc, em, dram["x2"], dram["x2o"], T, "cp2")
        ffn_call(dram["x2"], out_f, "c")

        em.begin()
        em.o("sp", lambda s: s.nop(), w=[("dx", em.c("dx")), ("dg", em.c("dg"))])
        em.flush()

    return nc


def _copy_dram(nc, em, src, dst, T, tag):
    with nc.sbuf_tensor(f"{tag}buf", [128, E], F32) as buf:
        em.begin()
        for i in range(T // 128):
            em.o("sp", lambda s, sl=slice(i * 128, (i + 1) * 128):
                 s.dma_start(out=buf[:, :], in_=src[sl, :]),
                 w=[("dx", em.c("dx"))], i=[("dx", 16)])
            em.o("sp", lambda s, sl=slice(i * 128, (i + 1) * 128), d=dst:
                 s.dma_start(out=d[sl, :], in_=buf[:, :]),
                 w=[("dx", em.c("dx"))], i=[("dx", 16)])
        em.o("sp", lambda s: s.nop(), w=[("dx", em.c("dx"))])
        em.flush()


def host_inputs(inputs, T):
    c = make_consts(T)

    def t(a):
        return np.ascontiguousarray(np.asarray(a).T)

    def pcol(a):
        return np.ascontiguousarray(np.asarray(a).reshape(EC, 128).T)

    m = {
        "fqw": t(inputs["L0_ffn_q_w"]), "fkw": t(inputs["L0_ffn_k_w"]),
        "fvw": t(inputs["L0_ffn_v_w"]), "ffw": t(inputs["L0_ffn_ffn_w"]),
        "few": np.ascontiguousarray(np.asarray(inputs["L0_ffn_expert_w"]).transpose(0, 2, 1)),
        "fgw": t(inputs["L0_ffn_gate_w"]),
        "fqb": np.asarray(inputs["L0_ffn_q_b"]).reshape(1, E),
        "fkb": np.asarray(inputs["L0_ffn_k_b"]).reshape(1, E),
        "fvb": np.asarray(inputs["L0_ffn_v_b"]).reshape(1, E),
        "ffb": np.asarray(inputs["L0_ffn_ffn_b"]).reshape(1, E),
        "fgb": np.asarray(inputs["L0_ffn_gate_b"]).reshape(1, NE),
        "feb": np.ascontiguousarray(np.asarray(inputs["L0_ffn_expert_b"])),
        "mwin": np.ascontiguousarray(np.asarray(inputs["L0_attn_W_in"])),
        "mwq": np.ascontiguousarray(np.asarray(inputs["L0_attn_Wq"])),
        "mwk": np.ascontiguousarray(np.asarray(inputs["L0_attn_Wk"])),
        "mwout": np.ascontiguousarray(np.asarray(inputs["L0_attn_W_out"])),
        "mbq": np.asarray(inputs["L0_attn_bq"]).reshape(1, DEXP),
        "mbk": np.asarray(inputs["L0_attn_bk"]).reshape(1, DEXP),
        "inlnw": pcol(inputs["L0_attn_in_ln_w"]), "inlnb": pcol(inputs["L0_attn_in_ln_b"]),
        "qlnw": pcol(inputs["L0_attn_q_ln_w"]), "qlnb": pcol(inputs["L0_attn_q_ln_b"]),
        "klnw": pcol(inputs["L0_attn_k_ln_w"]), "klnb": pcol(inputs["L0_attn_k_ln_b"]),
        "outlnw": np.asarray(inputs["L0_attn_out_ln_w"]).reshape(1, E),
        "outlnb": np.asarray(inputs["L0_attn_out_ln_b"]).reshape(1, E),
        "smear": np.asarray(inputs["L0_attn_smear_factor"]).reshape(1, 16),
        "logscale": np.asarray(inputs["L0_attn_log_scale"]).reshape(1, 16),
    }
    m.update(c)
    m = {k: np.ascontiguousarray(np.asarray(v, dtype=np.float32)) for k, v in m.items()}
    return m


def kernel(**inputs):
    x = np.asarray(inputs["L0_attn_x"], dtype=np.float32)
    B, T, _ = x.shape
    base = host_inputs(inputs, T)
    nc = build_program(T)
    in_maps = []
    for b in range(B):
        mm = dict(base)
        mm["x"] = np.ascontiguousarray(x[b])
        in_maps.append(mm)
    res = run_bass_kernel_spmd(nc, in_maps, core_ids=list(range(B)))
    return np.stack([np.asarray(r["out"], dtype=np.float32) for r in res.results], axis=0)


# version 05 (final)
# speedup vs baseline: 1.3365x; 1.3365x over previous
import math
import contextlib
from collections import defaultdict

import numpy as np
import ml_dtypes

import concourse.bass as bass
import concourse.mybir as mybir
from concourse.bass_utils import run_bass_kernel_spmd

F32 = mybir.dt.float32
BF16 = mybir.dt.bfloat16
AF = mybir.ActivationFunctionType
ALU = mybir.AluOpType
AX = mybir.AxisListType
BFNP = ml_dtypes.bfloat16

E = 1024
EC = 8
HF = 16
DF = 64
DEXP = 2048
NE = 8
LN_EPS = 1e-5
ATTN_EPS = 1e-5
EXP_SHIFT = -10.0
SM_SCALE = 0.125
EPS_SH = ATTN_EPS * math.exp(EXP_SHIFT)
LN_HALF = math.log(0.5)

_slopes = 2.0 ** np.linspace(0.0, -8.0, 8)
LAMB = [float(np.exp(-s)) for s in _slopes] + [1.0] * 8
LAMC = [l ** 128 for l in LAMB]

ENG2BLK = {"pe": "tensor", "act": "scalar", "dve": "vector", "sp": "sync", "gp": "gpsimd"}


def make_consts(T):
    TI = T // 128
    inv = 1.0 / (10000.0 ** (np.arange(0, DF, 2, dtype=np.float64) / DF))
    pos = np.arange(T, dtype=np.float64)
    fr = np.outer(pos, inv)
    emb = np.concatenate([fr, fr], axis=1)
    cos = np.cos(emb)
    sinf = np.sin(emb)
    sinf[:, :32] *= -1.0

    def tab(a):
        return np.ascontiguousarray(
            a.reshape(TI, 128, DF).transpose(1, 0, 2).reshape(128, TI * DF)
        ).astype(np.float32)

    jj = np.arange(128, dtype=np.float64)[:, None]
    ii = np.arange(128, dtype=np.float64)[None, :]
    masks = []
    lamqf = []
    with np.errstate(under="ignore"):
        for h in range(16):
            lam = LAMB[h]
            masks.append(np.where(ii >= jj, lam ** np.maximum(ii - jj, 0.0), 0.0))
            lamqf.append(np.broadcast_to(lam ** (np.arange(128, dtype=np.float64) + 1), (128, 128)))
        maskT = np.concatenate(masks, axis=1).astype(np.float32)
        lamq_feat = np.concatenate(lamqf, axis=1).astype(np.float32)
        lamk = np.stack(
            [LAMB[h] ** (127 - np.arange(128, dtype=np.float64)) for h in range(16)], axis=1
        ).astype(np.float32)
    ident = np.eye(128, dtype=np.float32)
    cpool = np.zeros((128, 8), np.float32)
    for j, vv in enumerate([LN_EPS, -0.5, LN_HALF, SM_SCALE, EXP_SHIFT, -1.0, 0.0, 1.0]):
        cpool[:, j] = vv
    return {
        "cos_t": tab(cos), "sinf_t": tab(sinf), "maskT": maskT,
        "lamq_feat": lamq_feat, "lamk": lamk, "ident": ident, "cpool": cpool,
    }


ENG_SEM = {"pe": "pe", "act": "act", "dve": "dve", "sp": "dx", "gp": "dg"}
ENG_INC = {"pe": 1, "act": 1, "dve": 1, "sp": 16, "gp": 16}


class Em:
    def __init__(self, nc, paranoid=()):
        self.nc = nc
        self.pset = paranoid if not isinstance(paranoid, bool) else (
            {"a1","a2","a3","m0","mp","ms","c1","c2","c3","tap","glob"} if paranoid else set())
        self.paranoid = False
        self.sems = {}
        self.cnt = defaultdict(int)
        self.mark = {}
        self.ops = None

    def phase(self, tag):
        self.paranoid = tag in self.pset

    def sem(self, name):
        if name not in self.sems:
            self.sems[name] = self.nc.alloc_semaphore(name)
        return self.sems[name]

    def begin(self):
        assert self.ops is None
        self.ops = {e: [] for e in ENG2BLK}
        self._fence = [("dx", self.cnt["dx"]), ("dg", self.cnt["dg"])]

    def o(self, eng, fn, w=(), i=()):
        w = list(w)
        i = list(i)
        if self.paranoid:
            w = [(s, self.cnt[s]) for s in ("pe", "act", "dve", "dx", "dg")]
            if not any(s == ENG_SEM[eng] for s, _ in i):
                i = i + [(ENG_SEM[eng], ENG_INC[eng])]
        for s, d in i:
            self.cnt[s] += d
        self.ops[eng].append((w, fn, i))

    def c(self, s):
        return self.cnt[s]

    def _validate(self, ops):
        import copy
        counts = dict(self._vcounts) if hasattr(self, "_vcounts") else {}
        heads = {e: 0 for e in ops}
        total = sum(len(l) for l in ops.values())
        done = 0
        while done < total:
            progressed = False
            for e, lst in ops.items():
                while heads[e] < len(lst):
                    waits, fn, incs = lst[heads[e]]
                    if all(counts.get(s, 0) >= v for s, v in waits):
                        for s, d in incs:
                            counts[s] = counts.get(s, 0) + d
                        heads[e] += 1
                        done += 1
                        progressed = True
                    else:
                        break
            if not progressed:
                msg = []
                for e, lst in ops.items():
                    if heads[e] < len(lst):
                        waits, fn, incs = lst[heads[e]]
                        bad = [(s, v, counts.get(s, 0)) for s, v in waits
                               if counts.get(s, 0) < v]
                        co = fn.__code__
                        msg.append(f"{e}[{heads[e]}/{len(lst)}] at "
                                   f"{co.co_filename}:{co.co_firstlineno} waits {bad}")
                raise RuntimeError("plan deadlock:\n" + "\n".join(msg))
        self._vcounts = counts

    def flush(self):
        ops = self.ops
        self.ops = None
        for eng, lst in ops.items():
            if lst:
                w, fn, i = lst[0]
                lst[0] = ([f for f in self._fence if f[1] > 0] + w, fn, i)
        self._validate(ops)
        with self.nc.allow_low_precision(reason="bf16 matmul operands by design"), \
                self.nc.Block() as blk:
            for eng, lst in ops.items():
                if not lst:
                    continue

                def body(e, lst=lst):
                    maxw = {}
                    for waits, fn, incs in lst:
                        for s, v in waits:
                            if v > maxw.get(s, 0):
                                e.wait_ge(self.sem(s), v)
                                maxw[s] = v
                        ins = fn(e)
                        for s, d in incs:
                            ins.then_inc(self.sem(s), d)

                getattr(blk, ENG2BLK[eng])(body)


def build_program(T, taps=(), paranoid=()):
    nc = bass.Bass("TRN2", target_bir_lowering=False, detect_race_conditions=False)
    TI = T // 128
    QB = T // 512
    taps = set(taps)
    dram = {}

    def din(name, shape):
        dram[name] = nc.dram_tensor(name, list(shape), F32, kind="ExternalInput").ap()

    def dscr(name, shape, dtype):
        dram[name] = nc.dram_tensor(name, list(shape), dtype, kind="Internal").ap()

    din("x", (T, E))
    out_f = nc.dram_tensor("out", [T, E], F32, kind="ExternalOutput").ap()
    for n, sh in [
        ("fqw", (E, E)), ("fkw", (E, E)), ("fvw", (E, E)), ("ffw", (E, E)),
        ("few", (NE, E, E)), ("fgw", (E, NE)),
        ("fqb", (1, E)), ("fkb", (1, E)), ("fvb", (1, E)), ("ffb", (1, E)),
        ("fgb", (1, NE)), ("feb", (NE, E)),
        ("mwin", (E, 2 * DEXP)), ("mwq", (E, DEXP)), ("mwk", (E, DEXP)),
        ("mwout", (DEXP, E)), ("mbq", (1, DEXP)), ("mbk", (1, DEXP)),
        ("inlnw", (128, EC)), ("inlnb", (128, EC)), ("qlnw", (128, EC)),
        ("qlnb", (128, EC)), ("klnw", (128, EC)), ("klnb", (128, EC)),
        ("outlnw", (1, E)), ("outlnb", (1, E)),
        ("smear", (1, 16)), ("logscale", (1, 16)),
        ("cos_t", (128, TI * DF)), ("sinf_t", (128, TI * DF)),
        ("maskT", (128, 16 * 128)), ("lamq_feat", (128, 16 * 128)),
        ("lamk", (128, 16)), ("ident", (128, 128)), ("cpool", (128, 8)),
    ]:
        din(n, sh)

    dscr("x1", (T, E), F32)
    dscr("x2", (T, E), F32)
    dscr("qT_s", (EC, 128, T), BF16)
    dscr("kT_s", (EC, 128, T), BF16)
    dscr("v_s", (EC, 128, TI * 130), BF16)
    dscr("aT_s", (EC, 128, T), BF16)
    dscr("y_s", (T, E), BF16)
    for n in ["mqhT", "mkhT", "mktl", "mvtk", "mphT"]:
        dscr(n, (TI, 128, DEXP), BF16)
    if "x1" in taps:
        dram["x1o"] = nc.dram_tensor("x1o", [T, E], F32, kind="ExternalOutput").ap()
    if "x2" in taps:
        dram["x2o"] = nc.dram_tensor("x2o", [T, E], F32, kind="ExternalOutput").ap()
    for tp, srcn in [("aqT", "qT_s"), ("akT", "kT_s"), ("av", "v_s"), ("aaT", "aT_s")]:
        if tp in taps:
            shp = [EC, 128, dram[srcn].shape[2]]
            dram[tp + "o"] = nc.dram_tensor(tp + "o", shp, BF16, kind="ExternalOutput").ap()
    if "ay" in taps:
        dram["ayo"] = nc.dram_tensor("ayo", [T, E], BF16, kind="ExternalOutput").ap()

    em = Em(nc, paranoid=paranoid)
    top = contextlib.ExitStack()

    with top:
        ident_b = top.enter_context(nc.sbuf_tensor("g_ident", [128, 128], BF16))
        onesb = top.enter_context(nc.sbuf_tensor("g_onesb", [1, 128], BF16))
        onesf = top.enter_context(nc.sbuf_tensor("g_onesf", [1, 128], F32))
        onescol = top.enter_context(nc.sbuf_tensor("g_onescol", [128, 1], BF16))
        cpool = top.enter_context(nc.sbuf_tensor("g_cpool", [128, 8], F32))

        nc.const_aps.register(F32, LN_EPS, cpool[:, 0:1])
        nc.const_aps.register(F32, -0.5, cpool[:, 1:2])
        nc.const_aps.register(F32, LN_HALF, cpool[:, 2:3])
        nc.const_aps.register(F32, SM_SCALE, cpool[:, 3:4])
        nc.const_aps.register(F32, EXP_SHIFT, cpool[:, 4:5])
        nc.const_aps.register(F32, -1.0, cpool[:, 5:6])
        nc.const_aps.register(F32, 0.0, cpool[:, 6:7])
        nc.const_aps.register(F32, 1.0, cpool[:, 7:8])

        em.begin()
        em.o("gp", lambda g: g.dma_start(out=ident_b[:, :], in_=dram["ident"][:, :]), i=[("dg", 16)])
        em.o("sp", lambda s: s.dma_start(out=cpool[:, :], in_=dram["cpool"][:, :]), i=[("dx", 16)])
        em.o("dve", lambda v: v.memset(onesb[:, :], 1.0))
        em.o("dve", lambda v: v.memset(onesf[:, :], 1.0))
        em.o("dve", lambda v: v.memset(onescol[:, :], 1.0), i=[("dve", 1)])
        em.flush()

        def ln_chain(xsb, stat, sqscr, rstd_bias, wx):
            mu, musq, ssq, rstd = (stat[:, j:j + 1] for j in range(4))
            em.o("dve", lambda v: v.reduce_sum(mu, xsb[:, :], axis=AX.X), w=wx, i=[("dve", 1)])
            em.o("dve", lambda v: v.tensor_scalar(mu, mu, 1.0 / E, None, ALU.mult),
                 w=[("dve", em.c("dve"))])
            em.o("dve", lambda v: v.scalar_tensor_tensor(
                sqscr[:, :], xsb[:, :], 1.0, xsb[:, :], ALU.mult, ALU.mult, accum_out=ssq),
                 i=[("dve", 1)])
            em.o("dve", lambda v: v.tensor_tensor(musq, mu, mu, ALU.mult))
            em.o("dve", lambda v: v.scalar_tensor_tensor(
                ssq, ssq, 1.0 / E, musq, ALU.mult, ALU.subtract),
                 w=[("dve", em.c("dve"))], i=[("dve", 1)])
            mv = em.c("dve")
            em.o("act", lambda s: s.activation(ssq, ssq, AF.Ln, bias=LN_EPS), w=[("dve", mv)])
            em.o("act", lambda s, bi=rstd_bias: s.activation(rstd, ssq, AF.Exp, scale=-0.5, bias=bi),
                 i=[("act", 1)])
            return em.c("act")

        def transpose_n(src_bf, ptr, dst_sb, n, pe_w=(), dve_w=()):
            key = ("ptrfree", ptr.name)
            pe_w = list(pe_w) + [("dve", em.mark.get(key, 0))]
            for j in range(n):
                em.o("pe", lambda t, o=ptr[:, j * 128:(j + 1) * 128],
                     s=src_bf[:, j * 128:(j + 1) * 128]: t.transpose(o, s, ident_b[:, :]),
                     w=(list(pe_w) if j == 0 else ()), i=([("pe", 1)] if j == n - 1 else []))
            mk_pe = em.c("pe")
            half = n * 128 // 2
            em.o("dve", lambda v: v.tensor_copy(dst_sb[:, 0:half], ptr[:, 0:half]),
                 w=[("pe", mk_pe)] + list(dve_w))
            em.o("dve", lambda v: v.tensor_copy(dst_sb[:, half:n * 128], ptr[:, half:n * 128]),
                 i=[("dve", 1)])
            em.mark[("ptrfree", ptr.name)] = em.c("dve")
            return mk_pe, em.c("dve")

        def transpose16(src_bf, ptr, dst_sb, pe_w=(), dve_w=()):
            mk1, dv1 = transpose_n(src_bf[:, 0:1024], ptr, dst_sb[:, 0:1024], 8, pe_w, dve_w)
            mk2, dv2 = transpose_n(src_bf[:, 1024:2048], ptr, dst_sb[:, 1024:2048], 8,
                                   [("dve", dv1)], ())
            return mk2, dv2

        def proj_mm(psum, yT, wsb, wcols, bias_row, pe_w=()):
            nh = (wcols + 511) // 512
            for half in range(nh):
                w0 = half * 512
                wn = min(512, wcols - w0)
                for c in range(EC):
                    em.o("pe", lambda t, o=psum[:, w0:w0 + wn],
                         l=yT[:, c * 128:(c + 1) * 128],
                         r=wsb[:, c * wcols + w0:c * wcols + w0 + wn],
                         st=(c == 0): t.matmul(o, l, r, start=st, stop=False),
                         w=(list(pe_w) if (half == 0 and c == 0) else ()))
                em.o("pe", lambda t, o=psum[:, w0:w0 + wn], r=bias_row[0:1, w0:w0 + wn]:
                     t.matmul(o, onesb[0:1, 0:128], r, start=False, stop=True),
                     i=([("pe", 1)] if half == nh - 1 else []))
            return em.c("pe")

        def wload(gname, dst_sb, nch, wcols, war=()):
            em.o("gp", lambda g, o=dst_sb, s=dram[gname].rearrange("(c p) f -> p c f", p=128):
                 g.dma_start(out=o[:, :].rearrange("p (c f) -> p c f", c=nch), in_=s),
                 w=list(war), i=[("dg", 16)])
            return em.c("dg")

        def ffn_call(xin_ap, xout_ap, tag):
            em.phase(tag + "1")
            ph = contextlib.ExitStack()
            with ph:
                wq_sb = ph.enter_context(nc.sbuf_tensor(f"{tag}wq", [128, EC * E], BF16))
                wk_sb = ph.enter_context(nc.sbuf_tensor(f"{tag}wk", [128, EC * E], BF16))
                wv_sb = ph.enter_context(nc.sbuf_tensor(f"{tag}wv", [128, EC * E], BF16))
                bq_sb = ph.enter_context(nc.sbuf_tensor(f"{tag}bq", [1, E], BF16))
                bk_sb = ph.enter_context(nc.sbuf_tensor(f"{tag}bk", [1, E], BF16))
                bv_sb = ph.enter_context(nc.sbuf_tensor(f"{tag}bv", [1, E], BF16))
                cos_sb = ph.enter_context(nc.sbuf_tensor(f"{tag}cos", [128, TI * DF], BF16))
                sin_sb = ph.enter_context(nc.sbuf_tensor(f"{tag}sin", [128, TI * DF], BF16))
                xsb = [ph.enter_context(nc.sbuf_tensor(f"{tag}xs{j}", [128, E], F32)) for j in range(2)]
                sqs = ph.enter_context(nc.sbuf_tensor(f"{tag}sqs", [128, E], F32))
                stat = ph.enter_context(nc.sbuf_tensor(f"{tag}st", [128, 4], F32))
                y_bf = [ph.enter_context(nc.sbuf_tensor(f"{tag}yb{j}", [128, E], BF16)) for j in range(2)]
                yT = [ph.enter_context(nc.sbuf_tensor(f"{tag}yT{j}", [128, E], BF16)) for j in range(2)]
                t1 = ph.enter_context(nc.sbuf_tensor(f"{tag}t1", [128, E], F32))
                t2 = ph.enter_context(nc.sbuf_tensor(f"{tag}t2", [128, E], F32))
                rof = [ph.enter_context(nc.sbuf_tensor(f"{tag}ro{j}", [128, E], BF16)) for j in range(2)]
                tT = [ph.enter_context(nc.sbuf_tensor(f"{tag}tT{j}", [128, E], BF16)) for j in range(2)]
                vst = [ph.enter_context(nc.sbuf_tensor(f"{tag}vs{j}", [128, HF * 65], BF16)) for j in range(2)]
                ptr = ph.enter_context(nc.psum_tensor(f"{tag}ptr", [128, E], BF16))
                pq = ph.enter_context(nc.psum_tensor(f"{tag}pq", [128, E], F32))
                pk = ph.enter_context(nc.psum_tensor(f"{tag}pk", [128, E], F32))
                pv = ph.enter_context(nc.psum_tensor(f"{tag}pv", [128, E], F32))

                em.begin()
                wload("fqw", wq_sb, EC, E)
                wload("fkw", wk_sb, EC, E)
                mk_w = wload("fvw", wv_sb, EC, E)
                for bsb, bn in [(bq_sb, "fqb"), (bk_sb, "fkb"), (bv_sb, "fvb")]:
                    em.o("gp", lambda g, o=bsb, s=dram[bn]: g.dma_start(out=o[:, :], in_=s[:, :]),
                         i=[("dg", 16)])
                em.o("gp", lambda g: g.dma_start(out=cos_sb[:, :], in_=dram["cos_t"][:, :]),
                     i=[("dg", 16)])
                em.o("gp", lambda g: g.dma_start(out=sin_sb[:, :], in_=dram["sinf_t"][:, :]),
                     i=[("dg", 16)])
                mk_w = em.c("dg")
                for j in range(2):
                    em.o("dve", lambda v, o=vst[j][:, :].rearrange("p (h c) -> p h c", h=HF)[:, :, 64:65]:
                         v.memset(o, 1.0), i=[("dve", 1)])

                def rope(psum, i_tile, dst_bf, pe_mk, war):
                    c0 = i_tile * DF
                    cos3 = cos_sb[:, c0:c0 + DF][:, None, :].broadcast_to([128, HF, DF])
                    sin_lo = sin_sb[:, c0:c0 + 32][:, None, :].broadcast_to([128, HF, 32])
                    sin_hi = sin_sb[:, c0 + 32:c0 + DF][:, None, :].broadcast_to([128, HF, 32])
                    ps3 = psum[:, :].rearrange("p (h d) -> p h d", h=HF)
                    t23 = t2[:, :].rearrange("p (h d) -> p h d", h=HF)
                    em.o("dve", lambda v: v.scalar_tensor_tensor(
                        t1[:, :].rearrange("p (h d) -> p h d", h=HF), ps3, 1.0, cos3,
                        ALU.mult, ALU.mult), w=[("pe", pe_mk)] + list(war))
                    em.o("dve", lambda v: v.scalar_tensor_tensor(
                        t23[:, :, 0:32], ps3[:, :, 32:64], 1.0, sin_lo, ALU.mult, ALU.mult))
                    em.o("dve", lambda v: v.scalar_tensor_tensor(
                        t23[:, :, 32:64], ps3[:, :, 0:32], 1.0, sin_hi, ALU.mult, ALU.mult))
                    em.o("dve", lambda v: v.tensor_tensor(
                        dst_bf[:, :], t1[:, :], t2[:, :], ALU.add), i=[("dve", 1)])
                    return em.c("dve")

                mk = defaultdict(int)
                for i in range(TI):
                    par = i % 2
                    em.o("sp", lambda s, o=xsb[par], src=dram["x"] if False else xin_ap,
                         sl=slice(i * 128, (i + 1) * 128): s.dma_start(out=o[:, :], in_=src[sl, :]),
                         w=[("dve", mk[f"xfree{par}"])], i=[("dx", 16)])
                    mk_x = em.c("dx")
                    actm = ln_chain(xsb[par], stat, sqs, 0.0, [("dx", mk_x), ("dve", mk[f"xfree{par}"])])
                    em.o("dve", lambda v, o=y_bf[par], a=xsb[par]: v.tensor_scalar(
                         o[:, :], a[:, :], stat[:, 0:1], stat[:, 3:4], ALU.subtract, ALU.mult),
                         w=[("act", actm), ("pe", mk[f"ybusy{par}"])], i=[("dve", 1)])
                    mk[f"xfree{par}"] = em.c("dve")
                    mk_y = em.c("dve")
                    if "ay" in taps:
                        em.o("sp", lambda s, o=dram["y_s"][i * 128:(i + 1) * 128, :],
                             src=y_bf[par]: s.dma_start(out=o, in_=src[:, :]),
                             w=[("dve", mk_y)], i=[("dx", 16)])
                    pem, dvem = transpose_n(y_bf[par], ptr, yT[par], 8,
                                            pe_w=[("dve", mk_y)],
                                            dve_w=[("pe", mk["yT_busy"])])
                    mk[f"ybusy{par}"] = pem
                    pe_q = proj_mm(pq, yT[par], wq_sb, E, bq_sb,
                                   pe_w=[("dve", dvem), ("dg", mk_w), ("dve", mk["ro_q"])])
                    mk["ro_q"] = rope(pq, i, rof[par], pe_q, [("pe", mk["trq"])])
                    pem, dvem_q = transpose_n(rof[par], ptr, tT[par], 8,
                                              pe_w=[("dve", mk["ro_q"])],
                                              dve_w=[("dx", mk[f"stq{par}"])])
                    mk["trq"] = pem
                    em.o("sp", lambda s, o=dram["qT_s"][:, :, i * 128:(i + 1) * 128].transpose([1, 0, 2]),
                         src=tT[par][:, :].rearrange("p (c t) -> p c t", c=EC):
                         s.dma_start(out=o, in_=src), w=[("dve", dvem_q)], i=[("dx", 16)])
                    mk[f"stq{par}"] = em.c("dx")
                    pe_k = proj_mm(pk, yT[par], wk_sb, E, bk_sb, pe_w=[("dve", mk["ro_k"])])
                    mk["ro_k"] = rope(pk, i, rof[par], pe_k, [("pe", mk["trq"]), ("dx", mk[f"stq{par}"])])
                    pem, dvem_k = transpose_n(rof[par], ptr, tT[par], 8,
                                              pe_w=[("dve", mk["ro_k"])],
                                              dve_w=[("dx", mk[f"stq{par}"])])
                    mk["trk"] = pem
                    mk["yT_busy"] = pe_k
                    em.o("sp", lambda s, o=dram["kT_s"][:, :, i * 128:(i + 1) * 128].transpose([1, 0, 2]),
                         src=tT[par][:, :].rearrange("p (c t) -> p c t", c=EC):
                         s.dma_start(out=o, in_=src), w=[("dve", dvem_k)], i=[("dx", 16)])
                    mk[f"stk{par}"] = em.c("dx")
                    pe_v = proj_mm(pv, yT[par], wv_sb, E, bv_sb, pe_w=[("dve", mk["vev"])])
                    mk["yT_busy"] = pe_v
                    em.o("dve", lambda v, o=vst[par][:, :].rearrange("p (h c) -> p h c", h=HF)[:, :, 0:64],
                         s=pv[:, :].rearrange("p (h d) -> p h d", h=HF): v.tensor_copy(o, s),
                         w=[("pe", pe_v), ("dx", mk[f"stv{par}"])], i=[("dve", 1)])
                    mk["vev"] = em.c("dve")
                    em.o("sp", lambda s, o=dram["v_s"][:, :, i * 130:(i + 1) * 130].transpose([1, 0, 2]),
                         src=vst[par][:, :].rearrange("p (c t) -> p c t", c=EC):
                         s.dma_start(out=o, in_=src), w=[("dve", mk["vev"])], i=[("dx", 16)])
                    mk[f"stv{par}"] = em.c("dx")
                em.flush()

            em.phase(tag + "2")
            ph = contextlib.ExitStack()
            with ph:
                kT_p = [ph.enter_context(nc.sbuf_tensor(f"{tag}kp{j}", [128, T], BF16)) for j in range(2)]
                v_p = [ph.enter_context(nc.sbuf_tensor(f"{tag}vp{j}", [128, TI * 130], BF16)) for j in range(2)]
                qTq = [ph.enter_context(nc.sbuf_tensor(f"{tag}qq{j}", [128, 512], BF16)) for j in range(2)]
                expS = [ph.enter_context(nc.sbuf_tensor(f"{tag}eS{j}", [128, 1024], BF16)) for j in range(2)]
                rr = ph.enter_context(nc.sbuf_tensor(f"{tag}rr", [1, 1024], F32))
                rb = ph.enter_context(nc.sbuf_tensor(f"{tag}rb", [1, 1024], BF16))
                rbig = ph.enter_context(nc.sbuf_tensor(f"{tag}rbig", [128, 512], F32))
                aTst = [ph.enter_context(nc.sbuf_tensor(f"{tag}aT{j}", [128, 512], BF16)) for j in range(2)]
                pS = [ph.enter_context(nc.psum_tensor(f"{tag}pS{j}", [128, 1024], F32)) for j in range(3)]
                pAV1 = ph.enter_context(nc.psum_tensor(f"{tag}pA1", [128, 512], F32))
                pAV2 = ph.enter_context(nc.psum_tensor(f"{tag}pA2", [128, 512], F32))

                em.begin()
                mk_kv = [0, 0]
                mk_exp = [0, 0, 0]
                mk_aTT = [0, 0]
                mk_aSt = [0, 0]
                mk_avdone = 0
                mk_rbTT = 0
                step = 0

                def load_kv(p, par):
                    em.o("gp", lambda g, o=kT_p[par], s=dram["kT_s"][p]:
                         g.dma_start(out=o[:, :], in_=s[:, :]), w=[("pe", mk_avdone)], i=[("dg", 16)])
                    em.o("gp", lambda g, o=v_p[par], s=dram["v_s"][p]:
                         g.dma_start(out=o[:, :], in_=s[:, :]), i=[("dg", 16)])
                    mk_kv[par] = em.c("dg")

                load_kv(0, 0)
                for p in range(EC):
                    par = p % 2
                    for qb in range(QB):
                        qpar = qb % 2
                        em.o("sp", lambda s, o=qTq[qpar], src=dram["qT_s"][p][:, qb * 512:(qb + 1) * 512]:
                             s.dma_start(out=o[:, :], in_=src), w=[("pe", mk_avdone)], i=[("dx", 16)])
                        mk_qT = em.c("dx")
                        if qb == 0 and p + 1 < EC:
                            load_kv(p + 1, 1 - par)
                        def emit_qk(tk, spar, epar):
                            em.o("pe", lambda t, o=pS[spar][:, 0:512],
                                 l=kT_p[par][0:64, tk * 128:(tk + 1) * 128],
                                 r=qTq[qpar][0:64, :]:
                                 t.matmul(o, l, r, start=True, stop=True, tile_position=(0, 0)),
                                 w=[("act", mk_exp[spar]), ("dg", mk_kv[par]), ("dx", mk_qT)])
                            em.o("pe", lambda t, o=pS[spar][:, 512:1024],
                                 l=kT_p[par][64:128, tk * 128:(tk + 1) * 128],
                                 r=qTq[qpar][64:128, :]:
                                 t.matmul(o, l, r, start=True, stop=True, tile_position=(64, 0)),
                                 i=[("pe", 1)])
                            mqk = em.c("pe")
                            em.o("act", lambda s, o=expS[epar], a=pS[spar]:
                                 s.activation(o[:, :], a[:, :], AF.Exp, scale=SM_SCALE, bias=EXP_SHIFT),
                                 w=[("pe", mqk)], i=[("act", 1)])
                            mk_exp[spar] = em.c("act")

                        def emit_av(tk, epar, act_mark):
                            wav = [("act", act_mark)]
                            if tk == 0:
                                wav += [("dve", max(mk_aTT))]
                            em.o("pe", lambda t, o=pAV1[0:65, :],
                                 l=v_p[par][:, tk * 130:tk * 130 + 65],
                                 r=expS[epar][:, 0:512], st=(tk == 0), sp=(tk == TI - 1):
                                 t.matmul(o, l, r, start=st, stop=sp), w=wav)
                            em.o("pe", lambda t, o=pAV2[0:65, :],
                                 l=v_p[par][:, tk * 130 + 65:tk * 130 + 130],
                                 r=expS[epar][:, 512:1024], st=(tk == 0), sp=(tk == TI - 1):
                                 t.matmul(o, l, r, start=st, stop=sp),
                                 i=([("pe", 1)] if tk == TI - 1 else []))

                        emit_qk(0, step % 3, step % 2)
                        prev_mark = mk_exp[step % 3]
                        for tk in range(TI):
                            if tk + 1 < TI:
                                emit_qk(tk + 1, (step + 1) % 3, (step + 1) % 2)
                            emit_av(tk, step % 2, prev_mark)
                            prev_mark = mk_exp[(step + 1) % 3] if tk + 1 < TI else 0
                            step += 1
                        mk_avdone = em.c("pe")
                        em.o("dve", lambda v: v.tensor_scalar(
                             rr[0:1, 0:512], pAV1[64:65, 0:512], EPS_SH, None, ALU.add),
                             w=[("pe", mk_avdone)])
                        em.o("dve", lambda v: v.tensor_scalar(
                             rr[0:1, 512:1024], pAV2[64:65, 0:512], EPS_SH, None, ALU.add),
                             i=[("dve", 1)])
                        em.o("dve", lambda v: v.reciprocal(rb[0:1, :], rr[0:1, :]),
                             w=[("dve", em.c("dve"))], i=[("dve", 1)])
                        mrec = em.c("dve")
                        bspar = step % 3
                        em.o("pe", lambda t, o=pS[bspar][0:64, 0:512]:
                             t.matmul(o, onesb[0:1, 0:64], rb[0:1, 0:512],
                             start=True, stop=True, tile_position=(0, 0)),
                             w=[("dve", mrec), ("act", mk_exp[bspar])])
                        em.o("pe", lambda t, o=pS[bspar][64:128, 0:512]:
                             t.matmul(o, onesb[0:1, 0:64], rb[0:1, 512:1024],
                             start=True, stop=True, tile_position=(0, 64)), i=[("pe", 1)])
                        mbc = em.c("pe")
                        em.o("act", lambda s, a=pS[bspar][:, 0:512]:
                             s.activation(rbig[:, :], a, AF.Copy),
                             w=[("pe", mbc), ("dve", mk_rbTT)], i=[("act", 1)])
                        mk_rb = em.c("act")
                        mk_exp[bspar] = mk_rb
                        apar = qb % 2
                        em.o("dve", lambda v, o=aTst[apar][0:64, :]:
                             v.tensor_tensor(o, pAV1[0:64, :], rbig[0:64, :], ALU.mult),
                             w=[("act", mk_rb), ("dx", mk_aSt[apar])])
                        em.o("dve", lambda v, o=aTst[apar][64:128, :]:
                             v.tensor_tensor(o, pAV2[0:64, :], rbig[64:128, :], ALU.mult),
                             i=[("dve", 1)])
                        mk_aTT[apar] = em.c("dve")
                        mk_rbTT = em.c("dve")
                        em.o("sp", lambda s, o=dram["aT_s"][p][:, qb * 512:(qb + 1) * 512],
                             src=aTst[apar]: s.dma_start(out=o, in_=src[:, :]),
                             w=[("dve", mk_aTT[apar])], i=[("dx", 16)])
                        mk_aSt[apar] = em.c("dx")
                em.flush()

            em.phase(tag + "3")
            ph = contextlib.ExitStack()
            with ph:
                T2 = T // 2
                TI2 = TI // 2
                aT_h = ph.enter_context(nc.sbuf_tensor(f"{tag}ah", [128, EC * T2], BF16))
                gw_sb = ph.enter_context(nc.sbuf_tensor(f"{tag}gw", [128, EC * NE], BF16))
                gb_sb = ph.enter_context(nc.sbuf_tensor(f"{tag}gb", [1, NE], BF16))
                we_sb = [ph.enter_context(nc.sbuf_tensor(f"{tag}we{j}", [128, EC * E], BF16)) for j in range(2)]
                web_sb = ph.enter_context(nc.sbuf_tensor(f"{tag}web", [1, NE * E], BF16))
                fw_sb = ph.enter_context(nc.sbuf_tensor(f"{tag}fw", [128, EC * E], BF16))
                fb_sb = ph.enter_context(nc.sbuf_tensor(f"{tag}fb", [1, E], BF16))
                moe = ph.enter_context(nc.sbuf_tensor(f"{tag}moe", [128, TI2 * E], F32))
                msel = ph.enter_context(nc.sbuf_tensor(f"{tag}ms", [128, TI2 * NE], F32))
                gtmp = ph.enter_context(nc.sbuf_tensor(f"{tag}gt", [128, 4 * NE], F32))
                gstat = ph.enter_context(nc.sbuf_tensor(f"{tag}gs", [128, 6], F32))
                moeb = ph.enter_context(nc.sbuf_tensor(f"{tag}mb", [128, E], BF16))
                moeT = ph.enter_context(nc.sbuf_tensor(f"{tag}mT", [128, E], BF16))
                xs3 = [ph.enter_context(nc.sbuf_tensor(f"{tag}x3{j}", [128, E], F32)) for j in range(2)]
                sqs3 = ph.enter_context(nc.sbuf_tensor(f"{tag}sq3", [128, E], F32))
                st3 = ph.enter_context(nc.sbuf_tensor(f"{tag}st3", [128, 4], F32))
                y05 = ph.enter_context(nc.sbuf_tensor(f"{tag}y05", [128, E], F32))
                fout = ph.enter_context(nc.sbuf_tensor(f"{tag}fo", [128, E], F32))
                pG = ph.enter_context(nc.psum_tensor(f"{tag}pG", [128, NE], F32))
                pE_ = [ph.enter_context(nc.psum_tensor(f"{tag}pE{j}", [128, E], F32)) for j in range(2)]
                ptr3 = ph.enter_context(nc.psum_tensor(f"{tag}p3", [128, E], BF16))
                pF = ph.enter_context(nc.psum_tensor(f"{tag}pF", [128, E], F32))

                em.begin()
                wload("fgw", gw_sb, EC, NE)
                em.o("gp", lambda g: g.dma_start(out=gb_sb[:, :], in_=dram["fgb"][:, :]), i=[("dg", 16)])
                em.o("gp", lambda g: g.dma_start(
                     out=web_sb[:, :].rearrange("p (n f) -> p n f", n=NE), in_=dram["feb"][None, :, :]),
                     i=[("dg", 16)])
                wload("ffw", fw_sb, EC, E)
                em.o("gp", lambda g: g.dma_start(out=fb_sb[:, :], in_=dram["ffb"][:, :]), i=[("dg", 16)])
                mk_w3 = em.c("dg")

                for half in range(2):
                    t0 = half * TI2
                    em.o("sp", lambda s, o=aT_h[:, :].rearrange("p (c t) -> p c t", c=EC),
                         src=dram["aT_s"][:, :, half * T2:(half + 1) * T2].transpose([1, 0, 2]):
                         s.dma_start(out=o, in_=src), w=[("pe", em.c("pe"))], i=[("dx", 16)])
                    mk_ah = em.c("dx")
                    em.phase(tag + "3g")
                    mk_pgf = 0
                    for tt in range(TI2):
                        for c in range(EC):
                            em.o("pe", lambda t, l=aT_h[:, c * T2 + tt * 128:c * T2 + (tt + 1) * 128],
                                 r=gw_sb[:, c * NE:(c + 1) * NE], st=(c == 0):
                                 t.matmul(pG[:, :], l, r, start=st, stop=False),
                                 w=([("dx", mk_ah), ("dg", mk_w3), ("act", mk_pgf)] if c == 0 else ()))
                        em.o("pe", lambda t: t.matmul(pG[:, :], onesb[0:1, 0:128], gb_sb[0:1, :],
                             start=False, stop=True), i=[("pe", 1)])
                        mg = em.c("pe")
                        gmx, gneg, gsum, grec, m1, m2 = (gstat[:, j:j + 1] for j in range(6))
                        ge, gh, g2, gm = (gtmp[:, j * NE:(j + 1) * NE] for j in range(4))
                        def dv(fn, w=()):
                            em.o("dve", fn, w=[("dve", em.c("dve"))] + list(w), i=[("dve", 1)])

                        dv(lambda v: v.reduce_max(gmx, pG[:, :], axis=AX.X), w=[("pe", mg)])
                        dv(lambda v: v.tensor_scalar(gneg, gmx, -1.0, None, ALU.mult))
                        md = em.c("dve")
                        em.o("act", lambda s: s.activation(ge, pG[:, :], AF.Exp, bias=gneg),
                             w=[("dve", md)], i=[("act", 1)])
                        mk_pgf = em.c("act")
                        dv(lambda v: v.reduce_sum(gsum, ge, axis=AX.X), w=[("act", mk_pgf)])
                        dv(lambda v: v.reciprocal(grec, gsum))
                        dv(lambda v: v.tensor_scalar(gh, ge, grec, None, ALU.mult))
                        dv(lambda v: v.reduce_max(m1, gh, axis=AX.X))
                        dv(lambda v: v.tensor_scalar(gm, gh, m1, None, ALU.is_ge))
                        dv(lambda v: v.scalar_tensor_tensor(g2, gm, -1e30, gh, ALU.mult, ALU.add))
                        dv(lambda v: v.reduce_max(m2, g2, axis=AX.X))
                        dv(lambda v: v.tensor_scalar(g2, g2, m2, None, ALU.is_ge))
                        dv(lambda v: v.tensor_tensor(gm, gm, g2, ALU.add))
                        dv(lambda v, o=msel[:, tt * NE:(tt + 1) * NE]:
                           v.tensor_tensor(o, gm, gh, ALU.mult))
                    mk_gate = em.c("dve")
                    em.phase(tag + "3e")
                    em.o("gp", lambda g, o=we_sb[0],
                         s=dram["few"][0].rearrange("(c p) f -> p c f", p=128):
                         g.dma_start(out=o[:, :].rearrange("p (c f) -> p c f", c=EC), in_=s),
                         w=[("pe", em.c("pe"))], i=[("dg", 16)])
                    mk_we = [em.c("dg"), 0]
                    mk_moe = [mk_gate] * TI2
                    mk_peu = [0, 0]
                    for n in range(NE):
                        npar = n % 2
                        if n + 1 < NE:
                            em.o("gp", lambda g, o=we_sb[1 - npar],
                                 s=dram["few"][n + 1].rearrange("(c p) f -> p c f", p=128):
                                 g.dma_start(out=o[:, :].rearrange("p (c f) -> p c f", c=EC), in_=s),
                                 w=[("pe", mk_peu[1 - npar])], i=[("dg", 16)])
                            mk_we[1 - npar] = em.c("dg")
                        for tt in range(TI2):
                            epar = tt % 2
                            for hh in range(2):
                                w0 = hh * 512
                                for c in range(EC):
                                    ww = ([("dg", mk_we[npar]), ("dve", mk_moe[tt])]
                                          if (hh == 0 and c == 0) else ())
                                    em.o("pe", lambda t, o=pE_[epar][:, w0:w0 + 512],
                                         l=aT_h[:, c * T2 + tt * 128:c * T2 + (tt + 1) * 128],
                                         r=we_sb[npar][:, c * E + w0:c * E + w0 + 512], st=(c == 0):
                                         t.matmul(o, l, r, start=st, stop=False), w=ww)
                                em.o("pe", lambda t, o=pE_[epar][:, w0:w0 + 512],
                                     r=web_sb[0:1, n * E + w0:n * E + w0 + 512]:
                                     t.matmul(o, onesb[0:1, 0:128], r, start=False, stop=True),
                                     i=([("pe", 1)] if hh == 1 else []))
                            mk_peu[epar] = em.c("pe")
                            mpe = em.c("pe")
                            if n == 0:
                                em.o("dve", lambda v, o=moe[:, tt * E:(tt + 1) * E],
                                     a=pE_[epar], s1=msel[:, tt * NE + n:tt * NE + n + 1]:
                                     v.tensor_scalar(o, a[:, :], s1, None, ALU.mult),
                                     w=[("pe", mpe)], i=[("dve", 1)])
                            else:
                                em.o("dve", lambda v, o=moe[:, tt * E:(tt + 1) * E],
                                     a=pE_[epar], s1=msel[:, tt * NE + n:tt * NE + n + 1]:
                                     v.scalar_tensor_tensor(o, a[:, :], s1, o, ALU.mult, ALU.add),
                                     w=[("pe", mpe)], i=[("dve", 1)])
                            mk_moe[tt] = em.c("dve")
                    em.phase(tag + "3f")
                    mk_fin = [0, 0]
                    mk_tr3 = 0
                    mk_mTu = 0
                    mk_fst = 0
                    for tt in range(TI2):
                        ig = t0 + tt
                        par = tt % 2
                        em.o("sp", lambda s, o=xs3[par], src=xin_ap,
                             sl=slice(ig * 128, (ig + 1) * 128): s.dma_start(out=o[:, :], in_=src[sl, :]),
                             w=[("dve", mk_fin[par])], i=[("dx", 16)])
                        mk_xl = em.c("dx")
                        em.o("dve", lambda v, a=moe[:, tt * E:(tt + 1) * E]:
                             v.tensor_copy(moeb[:, :], a), w=[("dve", mk_moe[tt]), ("pe", mk_tr3)],
                             i=[("dve", 1)])
                        mcast = em.c("dve")
                        pem, dvem = transpose_n(moeb, ptr3, moeT, 8, pe_w=[("dve", mcast)],
                                                dve_w=[("pe", mk_mTu)])
                        mk_tr3 = pem
                        pe_f = proj_mm(pF, moeT, fw_sb, E, fb_sb,
                                       pe_w=[("dve", dvem), ("dve", max(mk_fin))])
                        mk_mTu = pe_f
                        actm = ln_chain(xs3[par], st3, sqs3, LN_HALF, [("dx", mk_xl)])
                        em.o("dve", lambda v, a=xs3[par]: v.tensor_scalar(
                             y05[:, :], a[:, :], st3[:, 0:1], st3[:, 3:4], ALU.subtract, ALU.mult),
                             w=[("act", actm)], i=[("dve", 1)])
                        em.o("dve", lambda v: v.scalar_tensor_tensor(
                             fout[:, :], pF[:, :], 0.5, y05[:, :], ALU.mult, ALU.add),
                             w=[("pe", pe_f), ("dx", mk_fst)])
                        em.o("dve", lambda v, a=xs3[par]: v.tensor_tensor(
                             fout[:, :], fout[:, :], a[:, :], ALU.add), i=[("dve", 1)])
                        mk_fin[par] = em.c("dve")
                        em.o("sp", lambda s, o=xout_ap, src=fout,
                             sl=slice(ig * 128, (ig + 1) * 128): s.dma_start(out=o[sl, :], in_=src[:, :]),
                             w=[("dve", mk_fin[par])], i=[("dx", 16)])
                        mk_fst = em.c("dx")
                em.flush()

        def mixer_call(xin_ap, xout_ap):
            em.phase("m0")
            ph = contextlib.ExitStack()
            with ph:
                w_in = ph.enter_context(nc.sbuf_tensor("mwin_sb", [128, EC * 2 * DEXP], BF16))
                wq_sb = ph.enter_context(nc.sbuf_tensor("mwq_sb", [128, EC * DEXP], BF16))
                wk_sb = ph.enter_context(nc.sbuf_tensor("mwk_sb", [128, EC * DEXP], BF16))
                bin_bf = ph.enter_context(nc.sbuf_tensor("mbinb", [1, 2 * DEXP], BF16))
                bq_bf = ph.enter_context(nc.sbuf_tensor("mbqb", [1, DEXP], BF16))
                bk_bf = ph.enter_context(nc.sbuf_tensor("mbkb", [1, DEXP], BF16))
                sgr = ph.enter_context(nc.sbuf_tensor("msg", [128, 48], F32))

                fs = contextlib.ExitStack()
                with fs:
                    lnv = {}
                    for n in ["inlnw", "inlnb", "qlnw", "qlnb", "klnw", "klnb"]:
                        lnv[n] = fs.enter_context(nc.sbuf_tensor(f"m_{n}", [128, EC], F32))
                    lnb_bf = {}
                    for n in ["inlnb", "qlnb", "klnb"]:
                        lnb_bf[n] = fs.enter_context(nc.sbuf_tensor(f"mbf_{n}", [128, EC], BF16))
                    bacc = fs.enter_context(nc.sbuf_tensor("mbacc", [1, 2 * DEXP], F32))
                    wtmp = fs.enter_context(nc.sbuf_tensor("mwt", [128, 2 * DEXP], F32))
                    wrawb = fs.enter_context(nc.sbuf_tensor("mwrb", [128, 2 * DEXP], BF16))
                    srow = fs.enter_context(nc.sbuf_tensor("msr", [1, 16], F32))
                    pbr = fs.enter_context(nc.psum_tensor("mpbr", [1, 512], F32))
                    pbx = fs.enter_context(nc.psum_tensor("mpbx", [128, 512], F32))

                    em.begin()
                    for n in lnv:
                        em.o("sp", lambda s, o=lnv[n], nm=n: s.dma_start(out=o[:, :], in_=dram[nm][:, :]),
                             i=[("dx", 16)])
                    em.o("sp", lambda s: s.dma_start(out=srow[:, :], in_=dram["smear"][:, :]), i=[("dx", 16)])
                    mk_lnv = em.c("dx")
                    em.o("act", lambda s: s.activation(sgr[0:1, 0:16], srow[0:1, :], AF.Sigmoid),
                         w=[("dx", mk_lnv)], i=[("act", 1)])
                    em.o("sp", lambda s: s.dma_start(out=srow[:, :], in_=dram["logscale"][:, :]),
                         w=[("act", em.c("act"))], i=[("dx", 16)])
                    em.o("act", lambda s: s.activation(sgr[0:1, 32:48], srow[0:1, :], AF.Exp, scale=-1.0),
                         w=[("dx", em.c("dx"))], i=[("act", 1)])
                    mk_a2 = em.c("act")
                    em.o("dve", lambda v: v.tensor_scalar(sgr[0:1, 16:32], sgr[0:1, 0:16], -1.0, 1.0,
                         ALU.mult, ALU.add), w=[("act", mk_a2)], i=[("dve", 1)])
                    mk_sg = em.c("dve")
                    em.o("pe", lambda t: t.matmul(pbx[:, 0:48], onesf[0:1, :], sgr[0:1, 0:48],
                         start=True, stop=True), w=[("dve", mk_sg)], i=[("pe", 1)])
                    em.o("dve", lambda v: v.tensor_copy(sgr[:, 0:48], pbx[:, 0:48]),
                         w=[("pe", em.c("pe"))], i=[("dve", 1)])
                    for n in ["inlnb", "qlnb", "klnb"]:
                        em.o("dve", lambda v, o=lnb_bf[n], s=lnv[n]: v.tensor_copy(o[:, :], s[:, :]),
                             w=[("dx", mk_lnv)], i=[("dve", 1)])

                    def fold2(src, wdst, wcols, lnw, lnbname, bias_src, bf_out):
                        if bias_src is not None:
                            em.o("sp", lambda s, o=bacc[0:1, 0:wcols]:
                                 s.dma_start(out=o, in_=dram[bias_src][:, :]),
                                 w=[("dve", em.c("dve"))], i=[("dx", 16)])
                            mk_b0 = [("dx", em.c("dx"))]
                        else:
                            em.o("dve", lambda v: v.memset(bacc[0:1, 0:wcols], 0.0),
                                 i=[("dve", 1)])
                            mk_b0 = [("dve", em.c("dve"))]
                        mk_bprev = [("dve", 0)]
                        for c in range(EC):
                            em.o("gp", lambda g, o=wtmp[:, 0:wcols],
                                 s=dram[src][c * 128:(c + 1) * 128, :]: g.dma_start(out=o, in_=s),
                                 w=[("dve", em.c("dve")), ("pe", em.c("pe"))], i=[("dg", 16)])
                            mw = em.c("dg")
                            em.o("dve", lambda v, wc=wcols: v.tensor_copy(
                                 wrawb[:, 0:wc], wtmp[:, 0:wc]),
                                 w=[("dg", mw), ("dx", mk_lnv), ("pe", em.c("pe"))], i=[("dve", 1)])
                            mraw = em.c("dve")
                            em.o("dve", lambda v, o=wdst[:, c * wcols:(c + 1) * wcols],
                                 s1=lnv[lnw][:, c:c + 1], wc=wcols:
                                 v.tensor_scalar(o, wtmp[:, 0:wc], s1, None, ALU.mult),
                                 i=[("dve", 1)])
                            for sl in range(wcols // 512):
                                s0 = sl * 512
                                em.o("pe", lambda t, l=lnb_bf[lnbname][:, c:c + 1],
                                     r=wrawb[:, s0:s0 + 512]: t.matmul(
                                     pbr[0:1, 0:512], l, r, start=True, stop=True),
                                     w=[("dve", mraw)] + mk_b0 + mk_bprev, i=[("pe", 1)])
                                mpb = em.c("pe")
                                em.o("dve", lambda v, o=bacc[0:1, s0:s0 + 512]:
                                     v.tensor_tensor(o, pbr[0:1, 0:512], o, ALU.add),
                                     w=[("pe", mpb)], i=[("dve", 1)])
                                mk_bprev = [("dve", em.c("dve"))]
                        em.o("dve", lambda v, wc=wcols, o=bf_out: v.tensor_copy(
                             o[0:1, 0:wc], bacc[0:1, 0:wc]), i=[("dve", 1)])
                        return em.c("dve")

                    fold2("mwin", w_in, 2 * DEXP, "inlnw", "inlnb", None, bin_bf)
                    fold2("mwq", wq_sb, DEXP, "qlnw", "qlnb", "mbq", bq_bf)
                    fold2("mwk", wk_sb, DEXP, "klnw", "klnb", "mbk", bk_bf)
                    em.flush()

                em.phase("mp")
                ps = contextlib.ExitStack()
                with ps:
                    xz = [ps.enter_context(nc.sbuf_tensor(f"mxz{j}", [128, E], F32)) for j in range(2)]
                    y1 = ps.enter_context(nc.sbuf_tensor("my1", [128, E], F32))
                    z_bf = ps.enter_context(nc.sbuf_tensor("mzb", [128, E], BF16))
                    zT = [ps.enter_context(nc.sbuf_tensor(f"mzT{j}", [128, E], BF16)) for j in range(2)]
                    sqz = ps.enter_context(nc.sbuf_tensor("msqz", [128, E], F32))
                    stz = ps.enter_context(nc.sbuf_tensor("mstz", [128, 4], F32))
                    qe = ps.enter_context(nc.sbuf_tensor("mqe", [128, DEXP], F32))
                    hsum = ps.enter_context(nc.sbuf_tensor("mhs", [128, 32], F32))
                    qt_bf = ps.enter_context(nc.sbuf_tensor("mqt", [128, DEXP], BF16))
                    kt_bf = [ps.enter_context(nc.sbuf_tensor(f"mkt{j}", [128, DEXP], BF16)) for j in range(2)]
                    ksh = ps.enter_context(nc.sbuf_tensor("mksh", [128, DEXP], BF16))
                    ktmp = ps.enter_context(nc.sbuf_tensor("mktm", [128, DEXP], BF16))
                    ksm = ps.enter_context(nc.sbuf_tensor("mksm", [128, DEXP], BF16))
                    tbig = ps.enter_context(nc.sbuf_tensor("mtb", [128, DEXP], BF16))
                    pqk = ps.enter_context(nc.psum_tensor("mpqk", [128, DEXP], F32))
                    ptrm = ps.enter_context(nc.psum_tensor("mptr", [128, E], BF16))

                    em.begin()
                    mk = defaultdict(int)
                    mk_bb = 0
                    mk_sgr = 0
                    for c in range(TI):
                        par = c % 2
                        em.o("sp", lambda s, o=xz[par], src=xin_ap, sl=slice(c * 128, (c + 1) * 128):
                             s.dma_start(out=o[:, :], in_=src[sl, :]), w=[("dve", mk[f"xzf{par}"])],
                             i=[("dx", 16)])
                        mk_x = em.c("dx")
                        actm = ln_chain(xz[par], stz, sqz, 0.0, [("dx", mk_x)])
                        em.o("dve", lambda v, a=xz[par]: v.tensor_scalar(
                             y1[:, :], a[:, :], stz[:, 0:1], stz[:, 3:4], ALU.subtract, ALU.mult),
                             w=[("act", actm)], i=[("dve", 1)])
                        mk[f"xzf{par}"] = em.c("dve")
                        actm = ln_chain(y1, stz, sqz, 0.0, [("dve", em.c("dve"))])
                        em.o("dve", lambda v: v.tensor_scalar(
                             z_bf[:, :], y1[:, :], stz[:, 0:1], stz[:, 3:4], ALU.subtract, ALU.mult),
                             w=[("act", actm), ("pe", mk["ztr"])], i=[("dve", 1)])
                        mzb = em.c("dve")
                        pem, dvem = transpose_n(z_bf, ptrm, zT[par], 8, pe_w=[("dve", mzb)],
                                                dve_w=[("pe", mk["zT_busy"])])
                        mk["ztr"] = pem
                        mk_zT = dvem

                        def mproj(wsb, wcols, bias_bf, pe_w):
                            for hh in range(wcols // 512):
                                w0 = hh * 512
                                for cc in range(EC):
                                    em.o("pe", lambda t, o=pqk[:, w0:w0 + 512],
                                         l=zT[par][:, cc * 128:(cc + 1) * 128],
                                         r=wsb[:, cc * wcols + w0:cc * wcols + w0 + 512], st=(cc == 0):
                                         t.matmul(o, l, r, start=st, stop=False),
                                         w=(list(pe_w) if (hh == 0 and cc == 0) else ()))
                                em.o("pe", lambda t, o=pqk[:, w0:w0 + 512], r=bias_bf[0:1, w0:w0 + 512]:
                                     t.matmul(o, onesb[0:1, 0:128], r, start=False, stop=True),
                                     i=([("pe", 1)] if hh == wcols // 512 - 1 else []))
                            return em.c("pe")

                        def softdiv(dst_bf, extra_w=()):
                            em.o("dve", lambda v: v.reduce_sum(
                                 hsum[:, 0:16], qe[:, :].rearrange("p (h d) -> p h d", h=16), axis=AX.X),
                                 w=list(extra_w), i=[("dve", 1)])
                            em.o("dve", lambda v: v.reciprocal(hsum[:, 0:16], hsum[:, 0:16]),
                                 w=[("dve", em.c("dve"))], i=[("dve", 1)])
                            em.o("dve", lambda v: v.tensor_tensor(
                                 hsum[:, 16:32], hsum[:, 0:16], sgr[:, 32:48], ALU.mult),
                                 w=[("dve", em.c("dve"))], i=[("dve", 1)])
                            em.o("dve", lambda v, o=dst_bf: v.tensor_tensor(
                                 o[:, :].rearrange("p (h d) -> p h d", h=16),
                                 qe[:, :].rearrange("p (h d) -> p h d", h=16),
                                 hsum[:, 16:32][:, :, None].broadcast_to([128, 16, 128]), ALU.mult),
                                 w=[("dve", em.c("dve"))], i=[("dve", 1)])
                            return em.c("dve")

                        mq = mproj(wq_sb, DEXP, bq_bf,
                                   [("dve", mk_zT), ("dve", mk_bb), ("act", mk["qe_act"]),
                                    ("dve", mk["qe_dve"])])
                        em.o("act", lambda s: s.activation(qe[:, :], pqk[:, :], AF.Exp),
                             w=[("pe", mq), ("dve", mk["qe_dve"])], i=[("act", 1)])
                        mk["qe_act"] = em.c("act")
                        mdq = softdiv(qt_bf, extra_w=[("act", mk["qe_act"]), ("pe", mk["qt_busy"]),
                                                      ("dve", mk_sgr)])
                        mk["qe_dve"] = mdq
                        pem, dvem = transpose16(qt_bf, ptrm, tbig, pe_w=[("dve", mdq)],
                                                dve_w=[("dx", mk["tbig_store"])])
                        mk["qt_busy"] = pem
                        mk["zT_busy"] = pem
                        em.o("sp", lambda s, o=dram["mqhT"][c], src=tbig:
                             s.dma_start(out=o[:, :], in_=src[:, :]), w=[("dve", dvem)], i=[("dx", 16)])
                        mk["st_q"] = em.c("dx")
                        mk["tbig_store"] = em.c("dx")
                        mkp = mproj(wk_sb, DEXP, bk_bf, [("act", mk["qe_act"]), ("dve", mk["qe_dve"])])
                        em.o("act", lambda s: s.activation(qe[:, :], pqk[:, :], AF.Exp),
                             w=[("pe", mkp), ("dve", mk["qe_dve"])], i=[("act", 1)])
                        mk["qe_act"] = em.c("act")
                        mdk = softdiv(kt_bf[par], extra_w=[("act", mk["qe_act"]), ("dg", mk["ksh_gp"]),
                                                           ("dve", mk["ksm_dve"])])
                        mk["qe_dve"] = mdk
                        em.o("gp", lambda g, o=ksh[1:128, :], s=kt_bf[par][0:127, :]:
                             g.dma_start(out=o, in_=s), w=[("dve", mdk), ("dve", mk["ksh_dve"])],
                             i=[("dg", 16)])
                        if c == 0:
                            em.o("dve", lambda v: v.memset(ksh[0:1, :], 0.0), i=[("dve", 1)])
                            mk_top = ("dve", em.c("dve"))
                        else:
                            em.o("gp", lambda g, o=ksh[0:1, :], s=kt_bf[1 - par][127:128, :]:
                                 g.dma_start(out=o, in_=s), i=[("dg", 16)])
                            mk_top = ("dg", em.c("dg"))
                        mk["ksh_gp"] = em.c("dg")
                        em.o("dve", lambda v: v.tensor_tensor(
                             ktmp[:, :].rearrange("p (h d) -> p h d", h=16),
                             ksh[:, :].rearrange("p (h d) -> p h d", h=16),
                             sgr[:, 0:16][:, :, None].broadcast_to([128, 16, 128]), ALU.mult),
                             w=[("dg", mk["ksh_gp"]), mk_top, ("dx", mk["st_ktl"])])
                        em.o("dve", lambda v, a=kt_bf[par][:, :].rearrange("p (h d) -> p h d", h=16):
                             v.tensor_tensor(
                             ksm[:, :].rearrange("p (h d) -> p h d", h=16), a,
                             sgr[:, 16:32][:, :, None].broadcast_to([128, 16, 128]), ALU.mult),
                             w=[("pe", mk["ksm_busy"])])
                        em.o("dve", lambda v: v.tensor_tensor(ksm[:, :], ksm[:, :], ktmp[:, :], ALU.add),
                             i=[("dve", 1)])
                        mk["ksh_dve"] = em.c("dve")
                        msm = em.c("dve")
                        em.o("dve", lambda v: v.tensor_tensor(
                             ktmp[:, :].rearrange("p (h d) -> p h d", h=16),
                             ksm[:, :].rearrange("p (h d) -> p h d", h=16),
                             dram_lamk_bc, ALU.mult), w=[("dx", mk["st_ktl"])], i=[("dve", 1)])
                        mktl = em.c("dve")
                        em.o("sp", lambda s, o=dram["mktl"][c], src=ktmp:
                             s.dma_start(out=o[:, :], in_=src[:, :]), w=[("dve", mktl)], i=[("dx", 16)])
                        mk["st_ktl"] = em.c("dx")
                        mk["tbig_store"] = em.c("dx")
                        mk["ksm_dve"] = mktl
                        pem, dvem = transpose16(ksm, ptrm, tbig,
                                                pe_w=[("dve", msm)],
                                                dve_w=[("dx", mk["tbig_store"])])
                        mk["ksm_busy"] = pem
                        em.o("sp", lambda s, o=dram["mkhT"][c], src=tbig:
                             s.dma_start(out=o[:, :], in_=src[:, :]), w=[("dve", dvem)], i=[("dx", 16)])
                        mk["st_k"] = em.c("dx")
                        mk["tbig_store"] = em.c("dx")
                        def mproj_vp(off, pe_w):
                            for hh in range(4):
                                w0 = hh * 512
                                for cc in range(EC):
                                    em.o("pe", lambda t, o=pqk[:, w0:w0 + 512],
                                         l=zT[par][:, cc * 128:(cc + 1) * 128],
                                         r=w_in[:, cc * 2 * DEXP + off + w0:cc * 2 * DEXP + off + w0 + 512],
                                         st=(cc == 0): t.matmul(o, l, r, start=st, stop=False),
                                         w=(list(pe_w) if (hh == 0 and cc == 0) else ()))
                                em.o("pe", lambda t, o=pqk[:, w0:w0 + 512],
                                     r=bin_bf[0:1, off + w0:off + w0 + 512]:
                                     t.matmul(o, onesb[0:1, 0:128], r, start=False, stop=True),
                                     i=([("pe", 1)] if hh == 3 else []))
                            return em.c("pe")

                        mv = mproj_vp(0, [("act", mk["qe_act"]), ("dve", mk["qe_dve"])])
                        em.o("dve", lambda v: v.tensor_copy(tbig[:, :], pqk[:, :]),
                             w=[("pe", mv), ("dx", mk["tbig_store"])], i=[("dve", 1)])
                        mvv = em.c("dve")
                        em.o("sp", lambda s, o=dram["mvtk"][c], src=tbig:
                             s.dma_start(out=o[:, :], in_=src[:, :]), w=[("dve", mvv)], i=[("dx", 16)])
                        mk["st_v"] = em.c("dx")
                        mk["tbig_store"] = em.c("dx")
                        mp = mproj_vp(DEXP, [("dve", mvv)])
                        em.o("act", lambda s: s.activation(qt_bf[:, :], pqk[:, :], AF.Silu),
                             w=[("pe", mp), ("pe", mk["qt_busy"]), ("dve", 0)], i=[("act", 1)])
                        mps = em.c("act")
                        pem, dvem = transpose16(qt_bf, ptrm, tbig,
                                                pe_w=[("act", mps)],
                                                dve_w=[("dx", mk["tbig_store"])])
                        mk["qt_busy"] = pem
                        em.o("sp", lambda s, o=dram["mphT"][c], src=tbig:
                             s.dma_start(out=o[:, :], in_=src[:, :]), w=[("dve", dvem)], i=[("dx", 16)])
                        mk["st_p"] = em.c("dx")
                        mk["tbig_store"] = em.c("dx")
                    em.flush()

            em.phase("ms")
            ph = contextlib.ExitStack()
            with ph:
                wo_sb = ph.enter_context(nc.sbuf_tensor("mwo", [128, HF * E], BF16))
                maskT_sb = ph.enter_context(nc.sbuf_tensor("mmask", [128, 16 * 128], F32))
                lamqf_sb = ph.enter_context(nc.sbuf_tensor("mlqf", [128, 16 * 128], BF16))
                olnw_bc = ph.enter_context(nc.sbuf_tensor("molw", [128, E], F32))
                olnb_bc = ph.enter_context(nc.sbuf_tensor("molb", [128, E], F32))
                orow = ph.enter_context(nc.sbuf_tensor("morow", [1, E], F32))
                qhT = [ph.enter_context(nc.sbuf_tensor(f"sqh{j}", [128, DEXP], BF16)) for j in range(2)]
                qlT = ph.enter_context(nc.sbuf_tensor("sql", [128, DEXP], BF16))
                khT = [ph.enter_context(nc.sbuf_tensor(f"skh{j}", [128, DEXP], BF16)) for j in range(2)]
                ktl = [ph.enter_context(nc.sbuf_tensor(f"skt{j}", [128, DEXP], BF16)) for j in range(2)]
                v_tok = [ph.enter_context(nc.sbuf_tensor(f"svt{j}", [128, DEXP], BF16)) for j in range(2)]
                phT = [ph.enter_context(nc.sbuf_tensor(f"sph{j}", [128, DEXP], BF16)) for j in range(2)]
                state_f = ph.enter_context(nc.sbuf_tensor("sstf", [128, 16 * 132], F32))
                state_b = ph.enter_context(nc.sbuf_tensor("sstb", [128, 16 * 132], BF16))
                smb = ph.enter_context(nc.sbuf_tensor("ssmb", [128, 128], BF16))
                rb2 = ph.enter_context(nc.sbuf_tensor("srb2", [128, 128], F32))
                otm = ph.enter_context(nc.sbuf_tensor("sotm", [128, 128], F32))
                oT = ph.enter_context(nc.sbuf_tensor("soT", [128, DEXP], BF16))
                o2 = ph.enter_context(nc.sbuf_tensor("so2", [128, E], F32))
                sqo = ph.enter_context(nc.sbuf_tensor("ssqo", [128, E], F32))
                sto = ph.enter_context(nc.sbuf_tensor("ssto", [128, 4], F32))
                xr = [ph.enter_context(nc.sbuf_tensor(f"sxr{j}", [128, E], F32)) for j in range(2)]
                xo = ph.enter_context(nc.sbuf_tensor("sxo", [128, E], F32))
                dtm = ph.enter_context(nc.sbuf_tensor("sdtm", [1, 128], F32))
                rden = ph.enter_context(nc.sbuf_tensor("srdn", [1, 128], BF16))
                pSl = ph.enter_context(nc.psum_tensor("spSl", [128, 128], F32))
                pOD = ph.enter_context(nc.psum_tensor("spOD", [128, 256], F32))
                pSt = ph.enter_context(nc.psum_tensor("spSt", [128, 132], F32))
                pWo = ph.enter_context(nc.psum_tensor("spWo", [128, E], F32))
                pBx = ph.enter_context(nc.psum_tensor("spBx", [128, 512], F32))

                em.begin()
                em.o("gp", lambda g, s=dram["mwout"].rearrange("(c p) f -> p c f", p=128):
                     g.dma_start(out=wo_sb[:, :].rearrange("p (c f) -> p c f", c=HF), in_=s),
                     i=[("dg", 16)])
                em.o("gp", lambda g: g.dma_start(out=maskT_sb[:, :], in_=dram["maskT"][:, :]),
                     i=[("dg", 16)])
                em.o("gp", lambda g: g.dma_start(out=lamqf_sb[:, :], in_=dram["lamq_feat"][:, :]),
                     i=[("dg", 16)])
                mk_cst = em.c("dg")
                em.o("sp", lambda s: s.dma_start(out=orow[:, :], in_=dram["outlnw"][:, :]),
                     i=[("dx", 16)])
                mk_or = em.c("dx")
                for hh, dst in [(0, olnw_bc), (1, olnw_bc)]:
                    pass
                for (row, dst) in [("outlnw", olnw_bc), ("outlnb", olnb_bc)]:
                    em.o("sp", lambda s, nm=row: s.dma_start(out=orow[:, :], in_=dram[nm][:, :]),
                         w=[("dve", em.c("dve"))], i=[("dx", 16)])
                    mk_or = em.c("dx")
                    for hh in range(2):
                        em.o("pe", lambda t, r=orow[0:1, hh * 512:(hh + 1) * 512]:
                             t.matmul(pBx[:, :], onesf[0:1, :], r, start=True, stop=True),
                             w=[("dx", mk_or), ("dve", em.c("dve"))], i=[("pe", 1)])
                        em.o("dve", lambda v, o=dst[:, hh * 512:(hh + 1) * 512]:
                             v.tensor_copy(o, pBx[:, :]), w=[("pe", em.c("pe"))], i=[("dve", 1)])
                em.o("dve", lambda v: v.memset(state_f[:, :], 0.0))
                em.o("dve", lambda v: v.memset(state_b[:, :], 0.0), i=[("dve", 1)])
                mk_st0 = em.c("dve")

                mk = defaultdict(int)
                mk_state = [mk_st0] * 16
                for c in range(TI):
                    par = c % 2
                    for nm, dst in [("mqhT", qhT[par]), ("mkhT", khT[par]), ("mktl", ktl[par]),
                                    ("mvtk", v_tok[par]), ("mphT", phT[par])]:
                        em.o("sp", lambda s, o=dst, src=dram[nm][c]:
                             s.dma_start(out=o[:, :], in_=src[:, :]),
                             w=[("pe", mk[f"busy{par}"]), ("dve", mk[f"busyd{par}"])], i=[("dx", 16)])
                    em.o("sp", lambda s, o=xr[par], src=xin_ap, sl=slice(c * 128, (c + 1) * 128):
                         s.dma_start(out=o[:, :], in_=src[sl, :]), w=[("dve", mk[f"xrf{par}"])],
                         i=[("dx", 16)])
                    mk_ld = em.c("dx")
                    em.o("dve", lambda v, a=qhT[par][:, :].rearrange("p (h d) -> p h d", h=16):
                         v.tensor_tensor(
                         qlT[:, :].rearrange("p (h d) -> p h d", h=16), a,
                         lamqf_sb[:, :].rearrange("p (h d) -> p h d", h=16), ALU.mult),
                         w=[("dx", mk_ld), ("dg", mk_cst), ("pe", mk["qlT_busy"])], i=[("dve", 1)])
                    mk_ql = em.c("dve")
                    for h in range(16):
                        em.o("pe", lambda t, l=khT[par][:, h * 128:(h + 1) * 128],
                             r=qhT[par][:, h * 128:(h + 1) * 128]:
                             t.matmul(pSl[:, :], l, r, start=True, stop=True),
                             w=[("dx", mk_ld), ("act", mk["rb2_act"]), ("dve", mk["sm_dve"])],
                             i=[("pe", 1)])
                        mk_sl = em.c("pe")
                        em.o("dve", lambda v, mh=maskT_sb[:, h * 128:(h + 1) * 128]:
                             v.tensor_tensor(smb[:, :], pSl[:, :], mh, ALU.mult),
                             w=[("pe", mk_sl)], i=[("dve", 1)])
                        mk["sm_dve"] = em.c("dve")
                        em.o("pe", lambda t, l=v_tok[par][:, h * 128:(h + 1) * 128]:
                             t.matmul(pOD[:, 0:128], l, smb[:, :], start=True, stop=False),
                             w=[("dve", mk["sm_dve"]), ("dve", mk["o_dve"])])
                        em.o("pe", lambda t, l=state_b[:, h * 132:h * 132 + 128],
                             r=qlT[:, h * 128:(h + 1) * 128]:
                             t.matmul(pOD[:, 0:128], l, r, start=False, stop=True),
                             w=[("dve", max(mk_ql, mk_state[h]))])
                        em.o("pe", lambda t: t.matmul(pOD[0:1, 128:256], onescol[:, :], smb[:, :],
                             start=True, stop=False))
                        em.o("pe", lambda t, l=state_b[:, h * 132 + 128:h * 132 + 129],
                             r=qlT[:, h * 128:(h + 1) * 128]:
                             t.matmul(pOD[0:1, 128:256], l, r, start=False, stop=True))
                        em.o("pe", lambda t, l=ktl[par][:, h * 128:(h + 1) * 128],
                             r=v_tok[par][:, h * 128:(h + 1) * 128]:
                             t.matmul(pSt[:, 0:128], l, r, start=True, stop=True),
                             w=[("dve", mk["stu_dve"])])
                        em.o("pe", lambda t, l=ktl[par][:, h * 128:(h + 1) * 128]:
                             t.matmul(pSt[:, 128:129], l, onescol[:, :], start=True, stop=True),
                             i=[("pe", 1)])
                        mk_pe_h = em.c("pe")
                        em.o("dve", lambda v, o=state_f[:, h * 132:h * 132 + 129], lc=LAMC[h]:
                             v.scalar_tensor_tensor(o, o, lc, pSt[:, 0:129], ALU.mult, ALU.add),
                             w=[("pe", mk_pe_h)])
                        em.o("dve", lambda v, o=state_b[:, h * 132:h * 132 + 129],
                             s=state_f[:, h * 132:h * 132 + 129]: v.tensor_copy(o, s),
                             i=[("dve", 1)])
                        mk_state[h] = em.c("dve")
                        mk["stu_dve"] = mk_state[h]
                        em.o("dve", lambda v: v.tensor_scalar(dtm[0:1, :], pOD[0:1, 128:256],
                             ATTN_EPS, None, ALU.add), w=[("pe", mk_pe_h)], i=[("dve", 1)])
                        em.o("dve", lambda v: v.reciprocal(rden[0:1, :], dtm[0:1, :]),
                             w=[("dve", em.c("dve"))], i=[("dve", 1)])
                        mk_rd = em.c("dve")
                        em.o("pe", lambda t: t.matmul(pSl[:, :], onesb[0:1, 0:128], rden[0:1, :],
                             start=True, stop=True), w=[("dve", mk_rd)], i=[("pe", 1)])
                        mk_bc = em.c("pe")
                        em.o("act", lambda s: s.activation(rb2[:, :], pSl[:, :], AF.Copy),
                             w=[("pe", mk_bc)], i=[("act", 1)])
                        mk["rb2_act"] = em.c("act")
                        em.o("dve", lambda v: v.tensor_tensor(otm[:, :], pOD[:, 0:128], rb2[:, :],
                             ALU.mult), w=[("act", mk["rb2_act"])])
                        em.o("dve", lambda v, o=oT[:, h * 128:(h + 1) * 128],
                             p=phT[par][:, h * 128:(h + 1) * 128]:
                             v.tensor_tensor(o, otm[:, :], p, ALU.mult), i=[("dve", 1)])
                        mk["o_dve"] = em.c("dve")
                    mk_oT = em.c("dve")
                    mk["qlT_busy"] = 0
                    for h in range(HF):
                        for hh in range(2):
                            em.o("pe", lambda t, o=pWo[:, hh * 512:(hh + 1) * 512],
                                 l=oT[:, h * 128:(h + 1) * 128],
                                 r=wo_sb[:, h * E + hh * 512:h * E + hh * 512 + 512],
                                 st=(h == 0), sp=(h == HF - 1): t.matmul(o, l, r, start=st, stop=sp),
                                 w=([("dve", mk_oT), ("dve", mk["o2_dve"]), ("dg", mk_cst)]
                                    if h == 0 and hh == 0 else ()),
                                 i=([("pe", 1)] if (h == HF - 1 and hh == 1) else []))
                    mk_wo = em.c("pe")
                    mk["qlT_busy"] = mk_wo
                    em.o("dve", lambda v: v.tensor_copy(o2[:, :], pWo[:, :]), w=[("pe", mk_wo)],
                         i=[("dve", 1)])
                    mo2 = em.c("dve")
                    actm = ln_chain(o2, sto, sqo, 0.0, [("dve", mo2)])
                    em.o("dve", lambda v: v.tensor_scalar(
                         sqo[:, :], o2[:, :], sto[:, 0:1], sto[:, 3:4], ALU.subtract, ALU.mult),
                         w=[("act", actm)])
                    em.o("dve", lambda v: v.tensor_tensor(sqo[:, :], sqo[:, :], olnw_bc[:, :],
                         ALU.mult))
                    em.o("dve", lambda v: v.tensor_tensor(sqo[:, :], sqo[:, :], olnb_bc[:, :],
                         ALU.add))
                    em.o("dve", lambda v, a=xr[par]: v.tensor_tensor(xo[:, :], sqo[:, :], a[:, :],
                         ALU.add), w=[("dx", mk["xo_st"])], i=[("dve", 1)])
                    mk[f"xrf{par}"] = em.c("dve")
                    mk["o2_dve"] = em.c("dve")
                    mxo = em.c("dve")
                    em.o("sp", lambda s, o=xout_ap, src=xo, sl=slice(c * 128, (c + 1) * 128):
                         s.dma_start(out=o[sl, :], in_=src[:, :]), w=[("dve", mxo)], i=[("dx", 16)])
                    mk["xo_st"] = em.c("dx")
                    mk[f"busy{par}"] = mk_wo
                    mk[f"busyd{par}"] = mxo
                em.flush()

        lamk_sb_holder = {}

        lamk_sb = top.enter_context(nc.sbuf_tensor("g_lamk", [128, 16], F32))
        em.begin()
        em.o("gp", lambda g: g.dma_start(out=lamk_sb[:, :], in_=dram["lamk"][:, :]), i=[("dg", 16)])
        em.flush()
        dram_lamk_bc = lamk_sb[:, :][:, :, None].broadcast_to([128, 16, 128])

        def _tap_scr(tp, srcn):
            if tp not in taps:
                return
            C = dram[srcn].shape[2]
            with nc.sbuf_tensor(f"tap{tp}", [128, C], BF16) as buf:
                em.begin()
                for cc in range(EC):
                    em.o("sp", lambda s, o=buf, si=dram[srcn][cc]:
                         s.dma_start(out=o[:, :], in_=si[:, :]),
                         w=[("dx", em.c("dx"))], i=[("dx", 16)])
                    em.o("sp", lambda s, o=dram[tp + "o"][cc], si=buf:
                         s.dma_start(out=o[:, :], in_=si[:, :]),
                         w=[("dx", em.c("dx"))], i=[("dx", 16)])
                em.o("sp", lambda s: s.nop(), w=[("dx", em.c("dx"))])
                em.flush()

        ffn_call(dram["x"], dram["x1"], "a")
        for tp, srcn in [("aqT", "qT_s"), ("akT", "kT_s"), ("av", "v_s"), ("aaT", "aT_s")]:
            _tap_scr(tp, srcn)
        if "ay" in taps:
            with nc.sbuf_tensor("tapy", [128, E], BF16) as buf:
                em.begin()
                for ti in range(TI):
                    em.o("sp", lambda s, o=buf, si=dram["y_s"][ti * 128:(ti + 1) * 128, :]:
                         s.dma_start(out=o[:, :], in_=si), w=[("dx", em.c("dx"))], i=[("dx", 16)])
                    em.o("sp", lambda s, o=dram["ayo"][ti * 128:(ti + 1) * 128, :], si=buf:
                         s.dma_start(out=o, in_=si[:, :]), w=[("dx", em.c("dx"))], i=[("dx", 16)])
                em.o("sp", lambda s: s.nop(), w=[("dx", em.c("dx"))])
                em.flush()
        if "x1" in taps:
            _copy_dram(nc, em, dram["x1"], dram["x1o"], T, "cp1")
        mixer_call(dram["x1"], dram["x2"])
        if "x2" in taps:
            _copy_dram(nc, em, dram["x2"], dram["x2o"], T, "cp2")
        ffn_call(dram["x2"], out_f, "c")

        em.begin()
        em.o("sp", lambda s: s.nop(), w=[("dx", em.c("dx")), ("dg", em.c("dg"))])
        em.flush()

    return nc


def _copy_dram(nc, em, src, dst, T, tag):
    with nc.sbuf_tensor(f"{tag}buf", [128, E], F32) as buf:
        em.begin()
        for i in range(T // 128):
            em.o("sp", lambda s, sl=slice(i * 128, (i + 1) * 128):
                 s.dma_start(out=buf[:, :], in_=src[sl, :]),
                 w=[("dx", em.c("dx"))], i=[("dx", 16)])
            em.o("sp", lambda s, sl=slice(i * 128, (i + 1) * 128), d=dst:
                 s.dma_start(out=d[sl, :], in_=buf[:, :]),
                 w=[("dx", em.c("dx"))], i=[("dx", 16)])
        em.o("sp", lambda s: s.nop(), w=[("dx", em.c("dx"))])
        em.flush()


def host_inputs(inputs, T):
    c = make_consts(T)

    def t(a):
        return np.ascontiguousarray(np.asarray(a).T)

    def pcol(a):
        return np.ascontiguousarray(np.asarray(a).reshape(EC, 128).T)

    m = {
        "fqw": t(inputs["L0_ffn_q_w"]), "fkw": t(inputs["L0_ffn_k_w"]),
        "fvw": t(inputs["L0_ffn_v_w"]), "ffw": t(inputs["L0_ffn_ffn_w"]),
        "few": np.ascontiguousarray(np.asarray(inputs["L0_ffn_expert_w"]).transpose(0, 2, 1)),
        "fgw": t(inputs["L0_ffn_gate_w"]),
        "fqb": np.asarray(inputs["L0_ffn_q_b"]).reshape(1, E),
        "fkb": np.asarray(inputs["L0_ffn_k_b"]).reshape(1, E),
        "fvb": np.asarray(inputs["L0_ffn_v_b"]).reshape(1, E),
        "ffb": np.asarray(inputs["L0_ffn_ffn_b"]).reshape(1, E),
        "fgb": np.asarray(inputs["L0_ffn_gate_b"]).reshape(1, NE),
        "feb": np.ascontiguousarray(np.asarray(inputs["L0_ffn_expert_b"])),
        "mwin": np.ascontiguousarray(np.asarray(inputs["L0_attn_W_in"])),
        "mwq": np.ascontiguousarray(np.asarray(inputs["L0_attn_Wq"])),
        "mwk": np.ascontiguousarray(np.asarray(inputs["L0_attn_Wk"])),
        "mwout": np.ascontiguousarray(np.asarray(inputs["L0_attn_W_out"])),
        "mbq": np.asarray(inputs["L0_attn_bq"]).reshape(1, DEXP),
        "mbk": np.asarray(inputs["L0_attn_bk"]).reshape(1, DEXP),
        "inlnw": pcol(inputs["L0_attn_in_ln_w"]), "inlnb": pcol(inputs["L0_attn_in_ln_b"]),
        "qlnw": pcol(inputs["L0_attn_q_ln_w"]), "qlnb": pcol(inputs["L0_attn_q_ln_b"]),
        "klnw": pcol(inputs["L0_attn_k_ln_w"]), "klnb": pcol(inputs["L0_attn_k_ln_b"]),
        "outlnw": np.asarray(inputs["L0_attn_out_ln_w"]).reshape(1, E),
        "outlnb": np.asarray(inputs["L0_attn_out_ln_b"]).reshape(1, E),
        "smear": np.asarray(inputs["L0_attn_smear_factor"]).reshape(1, 16),
        "logscale": np.asarray(inputs["L0_attn_log_scale"]).reshape(1, 16),
    }
    m.update(c)
    m = {k: np.ascontiguousarray(np.asarray(v, dtype=np.float32)) for k, v in m.items()}
    return m


def kernel(**inputs):
    x = np.asarray(inputs["L0_attn_x"], dtype=np.float32)
    B, T, _ = x.shape
    base = host_inputs(inputs, T)
    nc = build_program(T)
    in_maps = []
    for b in range(B):
        mm = dict(base)
        mm["x"] = np.ascontiguousarray(x[b])
        in_maps.append(mm)
    res = run_bass_kernel_spmd(nc, in_maps, core_ids=list(range(B)))
    return np.stack([np.asarray(r["out"], dtype=np.float32) for r in res.results], axis=0)
